# version 22
# baseline (speedup 1.0000x reference)
import numpy as np
import ml_dtypes
import concourse.bass as bass
import concourse.mybir as mybir
from concourse.bass_utils import run_bass_kernel_spmd

F32 = mybir.dt.float32
BF16 = mybir.dt.bfloat16
I32 = mybir.dt.int32
AF = mybir.ActivationFunctionType
ALU = mybir.AluOpType
AXX = mybir.AxisListType.X

REAL = dict(S=4096, D=4096, H=32, HKV=8)
E_EXPERTS = 64
EPS = 1e-5
THETA = 10000.0
MASKVAL = -30000.0
CSAFE = 1.05

ENG2BLOCK = {"pe": "tensor", "dve": "vector", "act": "scalar",
             "pool": "gpsimd", "sp": "sync"}


class Prog:

    def __init__(self, nc):
        self.nc = nc
        self.sem_ctx = []
        self.sems = {}
        self.count = {}
        for q, step in (("pe", 1), ("dve", 1), ("act", 1), ("pool", 1)):
            self._mk_queue(q, step)
        self.items = {e: [] for e in ENG2BLOCK}
        self.watermark = {e: {} for e in ENG2BLOCK}
        self.pending_barrier = {e: [] for e in ENG2BLOCK}

    def _mk_queue(self, q, step):
        guard = self.nc.semaphore(f"sem_{q}")
        sem = guard.__enter__()
        self.sem_ctx.append(guard)
        self.sems[q] = (sem, step)
        if not hasattr(self, "count") or self.count is None:
            self.count = {}
        self.count[q] = 0

    def q(self, name):
        if name not in self.sems:
            self._mk_queue(name, 16)
        return name

    def q1(self, name):
        if name not in self.sems:
            self._mk_queue(name, 1)
        return name

    def set_barrier(self, evs=None):
        evs = list(evs) if evs is not None else self.barrier_events()
        for e in ENG2BLOCK:
            self.pending_barrier[e] = evs

    def op(self, stream, fn, waits=(), tag=None):
        waits = list(waits)
        if self.pending_barrier[stream]:
            waits = self.pending_barrier[stream] + waits
            self.pending_barrier[stream] = []
        for w in waits:
            assert w is not None, "wait on untagged producer"
        ev = None
        if tag is not None:
            self.count[tag] += 1
            ev = (tag, self.count[tag])
        self.items[stream].append((fn, tuple(waits), tag))
        return ev

    def flush(self, block):
        for e in ENG2BLOCK:
            items = self.items[e]
            if not items:
                continue
            self.items[e] = []

            def body(eng, e=e, items=items):
                wm = self.watermark[e]
                for fn, waits, tagq in items:
                    for (q, v) in waits:
                        if v > wm.get(q, 0):
                            sem, step = self.sems[q]
                            eng.wait_ge(sem, v * step)
                            wm[q] = v
                    instr = fn(eng)
                    if tagq is not None:
                        sem, step = self.sems[tagq]
                        instr.then_inc(sem, step)
            getattr(block, ENG2BLOCK[e])(body)

    def barrier_events(self):
        return [(q, c) for q, c in self.count.items() if c > 0]


def _alloc(st, nc, name, shape, dtype, psum=False):
    f = nc.psum_tensor if psum else nc.sbuf_tensor
    return st.enter_context(f(name, shape, dtype))


def _derive(cfg):
    S, D, H, HKV = cfg["S"], cfg["D"], cfg["H"], cfg["HKV"]
    d = dict(cfg)
    d["ND"] = D // 128
    d["NH"] = H // 2
    d["NKV"] = HKV // 2
    d["G"] = H // HKV
    d["SQ2"] = S // 2
    d["NG"] = S // 512
    d["NT"] = S // 128
    d["NT2"] = d["SQ2"] // 128
    d["NDO"] = D // 512
    d["NC"] = S // 128
    d["NJ"] = S // 512
    d["NSLAB"] = (d["NH"] + 7) // 8
    d.setdefault("B", 4)
    return d


def _setup_common(p, nc, ctx, consts):
    t = {}
    t["cpool"] = ctx.enter_context(
        nc.sbuf_tensor("cpool", [128, max(1, len(consts))], F32))
    for i, cv in enumerate(consts):
        nc.const_aps.register(F32, cv, t["cpool"][:, i:i + 1])
    t["id_bf"] = ctx.enter_context(nc.sbuf_tensor("id_bf", [128, 128], BF16))
    t["id_f32"] = ctx.enter_context(nc.sbuf_tensor("id_f32", [128, 128], F32))
    t["ones_bf"] = ctx.enter_context(nc.sbuf_tensor("ones_bf", [128, 1], BF16))
    t["ones_f32r"] = ctx.enter_context(nc.sbuf_tensor("ones_f32r", [1, 128], F32))

    evs = []
    last = None
    for i, cv in enumerate(consts):
        last = p.op("pool", lambda g, ap=t["cpool"][:, i:i + 1], v=cv:
                    g.memset(ap, v), tag="pool")
    if last:
        evs.append(last)
    e1 = p.op("pool", lambda g: g.memset(t["id_bf"][:], 0.0), tag="pool")
    e2 = p.op("pool", lambda g: g.affine_select(
        out=t["id_bf"][:], in_=t["id_bf"][:], compare_op=ALU.not_equal,
        fill=1.0, base=0, pattern=[[-1, 128]], channel_multiplier=1),
        waits=[e1], tag="pool")
    e3 = p.op("pool", lambda g: g.memset(t["id_f32"][:], 0.0), tag="pool")
    e4 = p.op("pool", lambda g: g.affine_select(
        out=t["id_f32"][:], in_=t["id_f32"][:], compare_op=ALU.not_equal,
        fill=1.0, base=0, pattern=[[-1, 128]], channel_multiplier=1),
        waits=[e3], tag="pool")
    e5 = p.op("pool", lambda g: g.memset(t["ones_bf"][:], 1.0), tag="pool")
    e6 = p.op("pool", lambda g: g.memset(t["ones_f32r"][:], 1.0), tag="pool")
    evs += [e2, e4, e5, e6]
    t["ev_setup"] = evs
    return t



def build_fused(cfg, dbg_spills=False):
    cfg = _derive(cfg)
    S, D, ND, NH, NKV, G = (cfg[k] for k in ("S", "D", "ND", "NH", "NKV", "G"))
    NG, NC, NJ, NSLAB = (cfg[k] for k in ("NG", "NC", "NJ", "NSLAB"))
    H, SQ2, NT2, NDO = (cfg[k] for k in ("H", "SQ2", "NT2", "NDO"))
    E = E_EXPERTS
    DKV = NKV * 128
    invD = 1.0 / D

    nc = bass.Bass("TRN2", target_bir_lowering=False)
    x_in = nc.dram_tensor("x_in", [S, D], F32, kind="ExternalInput").ap()
    wq = nc.dram_tensor("wq", [D, NH * 128], F32, kind="ExternalInput").ap()
    wk = nc.dram_tensor("wk", [D, DKV], F32, kind="ExternalInput").ap()
    wv = nc.dram_tensor("wv", [D, DKV], F32, kind="ExternalInput").ap()
    x_my = nc.dram_tensor("x_my", [SQ2, D], F32, kind="ExternalInput").ap()
    wo = nc.dram_tensor("wo", [D, D], F32, kind="ExternalInput").ap()
    rwT = nc.dram_tensor("rwT", [D, E], F32, kind="ExternalInput").ap()
    rb = nc.dram_tensor("rb", [1, E], F32, kind="ExternalInput").ap()
    eb = nc.dram_tensor("eb", [E, D], F32, kind="ExternalInput").ap()
    y = nc.dram_tensor("y", [SQ2, D], F32, kind="ExternalOutput").ap()
    x1_s = nc.dram_tensor("x1_s", [SQ2, D], F32, kind="Internal").ap()
    send_s = nc.dram_tensor("send_s", [NH * 128, S], BF16, kind="Internal").ap()
    recv_t = nc.dram_tensor("recv_s", [H * 128, S], BF16, kind="Internal")
    recv_s = recv_t.ap()
    _spk = "ExternalOutput" if dbg_spills else "Internal"
    xfm_s = nc.dram_tensor("xfm_s", [D, S], BF16, kind=_spk).ap()
    kq_s = nc.dram_tensor("kq_s", [DKV, S], BF16, kind=_spk).ap()
    v_s = nc.dram_tensor("v_s", [S, DKV], BF16, kind=_spk).ap()
    q_s = nc.dram_tensor("q_s", [NH * 128, S], BF16, kind=_spk).ap()

    consts = [float(-np.log(THETA) / 64.0), float(-np.log(2 * np.pi)),
              float(-2.0 * np.pi), float(EPS), float(CSAFE * CSAFE / 128.0),
              float(1.0 / np.sqrt(128.0)), float(np.pi)]

    from contextlib import ExitStack
    with ExitStack() as octx, nc.Block() as block:
        p = Prog(nc)
        for gd in p.sem_ctx:
            octx.push(gd)
        t = _setup_common(p, nc, octx, consts)
        id_bf, id_f32 = t["id_bf"][:], t["id_f32"][:]
        ones_bf, ones_f32r = t["ones_bf"][:], t["ones_f32r"][:]
        ev_setup = t["ev_setup"]

        r_cols = octx.enter_context(nc.sbuf_tensor("r_cols", [128, cfg["NT"]], F32))
        mk_max = octx.enter_context(nc.sbuf_tensor("mk_max", [1, NKV], F32))
        mq_max = octx.enter_context(nc.sbuf_tensor("mq_max", [1, NH], F32))
        cbias = octx.enter_context(nc.sbuf_tensor("cbias", [128, NH], F32))

        with ExitStack() as p1ctx:
            cossin = p1ctx.enter_context(nc.sbuf_tensor("cossin", [128, S], F32))
            cossin2 = p1ctx.enter_context(nc.sbuf_tensor("cossin2", [128, S], F32))
            mt1 = p1ctx.enter_context(nc.sbuf_tensor("mt1", [128, 128], BF16))
            mt2 = p1ctx.enter_context(nc.sbuf_tensor("mt2", [128, 128], BF16))
            negmu_bf = p1ctx.enter_context(nc.sbuf_tensor("negmu_bf", [1, S], BF16))
            r_row = p1ctx.enter_context(nc.sbuf_tensor("r_row", [1, S], F32))
            s_wk = p1ctx.enter_context(nc.sbuf_tensor("s_wk", [1, DKV], BF16))
            s_wv = p1ctx.enter_context(nc.sbuf_tensor("s_wv", [1, DKV], BF16))
            s_wq = p1ctx.enter_context(nc.sbuf_tensor("s_wq", [1, NH * 128], BF16))

            em0 = p.op("pool", lambda g: g.memset(mt1[:], 0.0), tag="pool")
            em1 = p.op("pool", lambda g: g.affine_select(
                out=mt1[:], in_=mt1[:], compare_op=ALU.not_equal, fill=1.0,
                base=0, pattern=[[-1, 128]], channel_multiplier=1),
                waits=[em0], tag="pool")
            em2 = p.op("pool", lambda g: g.affine_select(
                out=mt1[:], in_=mt1[:], compare_op=ALU.not_equal, fill=-1.0,
                base=-64, pattern=[[-1, 128]], channel_multiplier=1),
                waits=[em1], tag="pool")
            em3 = p.op("pool", lambda g: g.memset(mt1[:, 64:128], 0.0),
                       waits=[em2], tag="pool")
            em4 = p.op("pool", lambda g: g.memset(mt2[:], 0.0), tag="pool")
            em5 = p.op("pool", lambda g: g.affine_select(
                out=mt2[:], in_=mt2[:], compare_op=ALU.not_equal, fill=1.0,
                base=64, pattern=[[-1, 128]], channel_multiplier=1),
                waits=[em4], tag="pool")
            em6 = p.op("pool", lambda g: g.affine_select(
                out=mt2[:], in_=mt2[:], compare_op=ALU.not_equal, fill=1.0,
                base=0, pattern=[[-1, 128]], channel_multiplier=1),
                waits=[em5], tag="pool")
            em7 = p.op("pool", lambda g: g.memset(mt2[:, 0:64], 0.0),
                       waits=[em6], tag="pool")
            ev_mt = [em3, em7]

            with ExitStack() as _st:
                rr = _st.enter_context(nc.sbuf_tensor("rr", [128, S], F32))
                work = _st.enter_context(nc.sbuf_tensor("work", [128, S], F32))
                flo = _st.enter_context(nc.sbuf_tensor("flo", [128, S], F32))
                ropei = _st.enter_context(nc.sbuf_tensor("ropei", [128, S], I32))
                jcol = _st.enter_context(nc.sbuf_tensor("jcol", [128, 1], F32))
                invf = _st.enter_context(nc.sbuf_tensor("invf", [128, 1], F32))
                ei1 = p.op("pool", lambda g: g.iota(
                    jcol[:], pattern=[[0, 1]], base=0, channel_multiplier=1,
                    allow_small_or_imprecise_dtypes=True), tag="pool")
                ei1b = p.op("pool", lambda g: g.iota(
                    jcol[64:128, :], pattern=[[0, 1]], base=0,
                    channel_multiplier=1,
                    allow_small_or_imprecise_dtypes=True),
                    waits=[ei1], tag="pool")
                ei2 = p.op("pool", lambda g: g.iota(
                    rr[:], pattern=[[1, S]], base=0, channel_multiplier=0,
                    allow_small_or_imprecise_dtypes=True), tag="pool")
                ea = p.op("act", lambda s: s.activation(
                    out=invf[:], in_=jcol[:], func=AF.Exp,
                    scale=consts[0], bias=consts[1]),
                    waits=[ei1, ei1b] + ev_setup, tag="act")
                ev1 = p.op("dve", lambda v: v.tensor_scalar_mul(
                    rr[:], rr[:], invf[:]), waits=[ea, ei2], tag="dve")
                ev_tab = None
                prev = [ev1]
                for dst, halfs in ((cossin, slice(0, 64)),
                                   (cossin2, slice(64, 128))):
                    a1 = p.op("dve", lambda v, dst=dst: v.tensor_copy(
                        work[:], rr[:]), waits=prev, tag="dve")
                    a2 = p.op("dve", lambda v, halfs=halfs: v.tensor_scalar_add(
                        work[halfs, :], work[halfs, :], 0.25),
                        waits=[a1], tag="dve")
                    a3 = p.op("dve", lambda v: v.tensor_copy(ropei[:], work[:]),
                              waits=[a2], tag="dve")
                    a4 = p.op("dve", lambda v: v.tensor_copy(flo[:], ropei[:]),
                              waits=[a3], tag="dve")
                    a5a = p.op("dve", lambda v: v.tensor_sub(
                        work[:], work[:], flo[:]), waits=[a4], tag="dve")
                    a5b = p.op("dve", lambda v: v.tensor_single_scalar(
                        flo[:], work[:], 0.0, ALU.is_lt), waits=[a5a], tag="dve")
                    a5 = p.op("dve", lambda v: v.tensor_add(
                        work[:], work[:], flo[:]), waits=[a5b], tag="dve")
                    ev_tab = p.op("act", lambda s, dst=dst: s.activation(
                        out=dst[:], in_=work[:], func=AF.Sin, scale=consts[2],
                        bias=consts[6]), waits=[a5], tag="act")
                    prev = [ev_tab, a5]
                p.flush(block)
            p.set_barrier()

            xfm_done = [None] * NG

            with ExitStack() as _st:
                wk_sb = _st.enter_context(nc.sbuf_tensor("wk_sb", [128, ND * 512], BF16))
                wv_sb = _st.enter_context(nc.sbuf_tensor("wv_sb", [128, ND * 512], BF16))
                xt = _st.enter_context(nc.sbuf_tensor("xt", [128, 2 * D], F32))
                xbf8 = _st.enter_context(nc.sbuf_tensor("xbf8", [128, 2 * 1024], BF16))
                xfmg = _st.enter_context(nc.sbuf_tensor("xfmg", [128, ND * 512], BF16))
                stat = _st.enter_context(nc.sbuf_tensor("stat", [128, 32], F32))
                trash_bf = _st.enter_context(nc.sbuf_tensor("trash_bf", [128, 1024], BF16))
                sq4 = _st.enter_context(nc.sbuf_tensor("sq4", [128, 16], F32))
                p1t = _st.enter_context(nc.sbuf_tensor("p1t", [128, 512], BF16))
                p2t = _st.enter_context(nc.sbuf_tensor("p2t", [128, 512], BF16))
                kbf = _st.enter_context(nc.sbuf_tensor("kbf", [128, 2 * 512], BF16))
                vbf = _st.enter_context(nc.sbuf_tensor("vbf", [128, 2 * 512], BF16))
                sqbf = _st.enter_context(nc.sbuf_tensor("sqbf", [128, 512], BF16))
                nmx = _st.enter_context(nc.sbuf_tensor("nmx", [1, 8], F32))
                rbc = _st.enter_context(nc.sbuf_tensor("rbc", [128, 512], F32))
                ps_tr = _st.enter_context(nc.psum_tensor("ps_tr", [128, 2 * 1024], BF16))
                ps_st = _st.enter_context(nc.psum_tensor("ps_st", [128, 512], F32))
                ps_rb = _st.enter_context(nc.psum_tensor("ps_rb", [128, 512], F32))
                ps_nrm = _st.enter_context(nc.psum_tensor("ps_nrm", [128, 512], F32))
                ps_kv = _st.enter_context(nc.psum_tensor("ps_kv", [128, 3 * 512], F32))

                wk_v = wk.rearrange("(d p) w -> p d w", p=128)
                wv_v = wv.rearrange("(d p) w -> p d w", p=128)
                ev_wk = p.op("pool", lambda g: g.dma_start(
                    out=wk_sb[:].rearrange("p (d w) -> p d w", w=512)[:, :, 0:DKV],
                    in_=wk_v), tag=p.q("qwk"))
                ev_wv = p.op("pool", lambda g: g.dma_start(
                    out=wv_sb[:].rearrange("p (d w) -> p d w", w=512)[:, :, 0:DKV],
                    in_=wv_v), tag=p.q("qwv"))
                ev_swk = None
                for d in range(ND):
                    ev_swk = p.op("pe", lambda te, d=d: te.matmul(
                        ps_st[0:1, 0:DKV], ones_bf[:],
                        wk_sb[:, d * 512:d * 512 + DKV],
                        start=(d == 0), stop=(d == ND - 1)),
                        waits=[ev_wk] + ev_setup,
                        tag="pe" if d == ND - 1 else None)
                ev_swk_c = p.op("dve", lambda v: v.tensor_copy(
                    s_wk[:], ps_st[0:1, 0:DKV]), waits=[ev_swk], tag="dve")
                ev_swv = None
                for d in range(ND):
                    ev_swv = p.op("pe", lambda te, d=d: te.matmul(
                        ps_nrm[0:1, 0:DKV], ones_bf[:],
                        wv_sb[:, d * 512:d * 512 + DKV],
                        start=(d == 0), stop=(d == ND - 1)),
                        waits=[ev_wv], tag="pe" if d == ND - 1 else None)
                ev_swv_c = p.op("dve", lambda v: v.tensor_copy(
                    s_wv[:], ps_nrm[0:1, 0:DKV]), waits=[ev_swv], tag="dve")
                ev_mki = p.op("dve", lambda v: v.memset(mk_max[:], 0.0), tag="dve")
                ev_mqi = p.op("dve", lambda v: v.memset(mq_max[:], 0.0), tag="dve")

                x_readers = {0: [], 1: []}
                xbf_readers = {0: [], 1: []}
                ps_tr_evac = {0: None, 1: None}
                ps_st_war = [ev_swk_c]
                ps_rb_war = []
                rbc_war = []
                ps_nrm_war = [ev_swv_c]
                kv_war = {0: [], 1: [], 2: []}
                kbf_war = {0: [], 1: []}
                vbf_war = {0: [], 1: []}
                nmx_war = [ev_mki, ev_mqi]
                pending_rope = [None]
                stat_fins = []
                trash_war = []
                sqbf_war = []
                p1_war = []
                p2_war = []
                rope_ps_war = [ev_swv_c]
                xfmg_war = []

                for g in range(NG):
                    ev_xfm_tiles = []
                    ev_r_tiles = []
                    ev_negmu_tiles = []
                    for tt in range(4):
                        ti = g * 4 + tt
                        par = ti % 2
                        p4 = ti % 4
                        xt_v = xt[:, par * D:(par + 1) * D]
                        ev_l = p.op("sp", lambda s, ti=ti, xt_v=xt_v: s.dma_start(
                            out=xt_v, in_=x_in[ti * 128:(ti + 1) * 128, :]),
                            waits=x_readers[par], tag=p.q(f"qx{par}"))
                        c_sum = stat[:, p4 * 8 + 0: p4 * 8 + 1]
                        c_sq = stat[:, p4 * 8 + 1: p4 * 8 + 2]
                        c_negmu = stat[:, p4 * 8 + 2: p4 * 8 + 3]
                        c_mu2 = stat[:, p4 * 8 + 3: p4 * 8 + 4]
                        c_var = stat[:, p4 * 8 + 4: p4 * 8 + 5]
                        c_sd = stat[:, p4 * 8 + 5: p4 * 8 + 6]
                        e1 = p.op("dve", lambda v, a=c_sum, b=xt_v: v.reduce_sum(
                            a, b, axis=AXX), waits=[ev_l], tag="dve")
                        q4w = min(1024, D)
                        nq4 = D // q4w
                        e2c = None
                        for q4 in range(nq4):
                            e2c = p.op("act", lambda s, q4=q4, b=xt_v, p4=p4,
                                       q4w=q4w: s.activation(
                                           out=trash_bf[:, 0:q4w],
                                           in_=b[:, q4 * q4w:(q4 + 1) * q4w],
                                           func=AF.Square,
                                           accum_out=sq4[:, p4 * 4 + q4:
                                                         p4 * 4 + q4 + 1]),
                                       waits=[ev_l] + trash_war, tag="act")
                            trash_war = [e2c]
                        e2 = p.op("dve", lambda v, a=c_sq, p4=p4: v.reduce_sum(
                            a, sq4[:, p4 * 4:p4 * 4 + nq4], axis=AXX),
                            waits=[e2c], tag="dve")
                        e3 = p.op("dve", lambda v, a=c_negmu, b=c_sum:
                                  v.tensor_scalar_mul(a, b, -invD),
                                  waits=[e1], tag="dve")
                        e4 = p.op("dve", lambda v, a=c_mu2, b=c_negmu:
                                  v.tensor_mul(a, b, b), waits=[e3], tag="dve")
                        e5 = p.op("dve", lambda v, a=c_var, b=c_sq, c=c_mu2:
                                  v.scalar_tensor_tensor(
                                      out=a, in0=b, scalar=invD, in1=c,
                                      op0=ALU.mult, op1=ALU.subtract),
                                  waits=[e2, e4], tag="dve")
                        e6 = p.op("act", lambda s, a=c_sd, b=c_var: s.activation(
                            out=a, in_=b, func=AF.Sqrt, bias=consts[3]),
                            waits=[e5], tag="act")
                        rc = r_cols[:, ti:ti + 1]
                        e7 = p.op("dve", lambda v, a=rc, b=c_sd: v.reciprocal(a, b),
                                  waits=[e6], tag="dve")

                        def stat_fin(ti=ti, e3=e3, e7=e7, c_negmu=c_negmu,
                                     rc=rc):
                            nonlocal ps_st_war
                            e8 = p.op("pe", lambda te, b=c_negmu: te.transpose(
                                ps_st[0:1, 0:128], b, id_f32),
                                waits=[e3] + ps_st_war, tag="pe")
                            e9 = p.op("dve", lambda v, ti=ti: v.tensor_copy(
                                negmu_bf[0:1, ti * 128:(ti + 1) * 128],
                                ps_st[0:1, 0:128]), waits=[e8], tag="dve")
                            e10 = p.op("pe", lambda te, b=rc: te.transpose(
                                ps_st[0:1, 128:256], b, id_f32),
                                waits=[e7, e9], tag="pe")
                            e11 = p.op("dve", lambda v, ti=ti: v.tensor_copy(
                                r_row[0:1, ti * 128:(ti + 1) * 128],
                                ps_st[0:1, 128:256]), waits=[e10], tag="dve")
                            ps_st_war = [e9, e11]
                            ev_r_tiles.append(e11)
                            ev_negmu_tiles.append(e9)
                        stat_fins.append(stat_fin)
                        x_cast_readers = []
                        NB8 = min(8, ND)
                        for bk in range(ND // NB8):
                            xbv = xbf8[:, (bk % 2) * 1024:
                                       (bk % 2) * 1024 + NB8 * 128]
                            ecast = p.op("dve", lambda v, a=xbv, b=xt_v, bk=bk,
                                         NB8=NB8: v.tensor_copy(
                                             a, b[:, bk * NB8 * 128:
                                                  (bk + 1) * NB8 * 128]),
                                         waits=[ev_l] + xbf_readers[bk % 2],
                                         tag="dve")
                            x_cast_readers.append(ecast)
                            bank = ps_tr[:, (bk % 2) * 1024:
                                         (bk % 2) * 1024 + NB8 * 128]
                            last_tr = None
                            for dd in range(NB8):
                                w = [ecast]
                                if dd == 0 and ps_tr_evac[bk % 2] is not None:
                                    w.append(ps_tr_evac[bk % 2])
                                last_tr = p.op("pe", lambda te, dd=dd,
                                               bank=bank, xbv=xbv:
                                               te.transpose(
                                                   bank[:, dd * 128:(dd + 1) * 128],
                                                   xbv[:, dd * 128:(dd + 1) * 128],
                                                   id_bf),
                                               waits=w,
                                               tag="pe" if dd == NB8 - 1 else None)
                            xbf_readers[bk % 2] = [last_tr]
                            dst = xfmg[:].rearrange("p (d w) -> p d w", w=512)[
                                :, bk * NB8:(bk + 1) * NB8,
                                tt * 128:(tt + 1) * 128]
                            eev = p.op("act", lambda s, bank=bank, dst=dst: s.copy(
                                out=dst,
                                in_=bank.rearrange("p (d w) -> p d w", w=128)),
                                waits=[last_tr] + xfmg_war, tag="act")
                            ps_tr_evac[bk % 2] = eev
                            ev_xfm_tiles.append(eev)
                        x_readers[par] = [e1, e2] + x_cast_readers

                    ev_xsp = p.op("sp", lambda s, g=g: s.dma_start(
                        out=xfm_s.rearrange("(d p) s -> p d s", p=128)[
                            :, :, g * 512:(g + 1) * 512],
                        in_=xfmg[:].rearrange("p (d w) -> p d w", w=512)),
                        waits=ev_xfm_tiles, tag=p.q("qxs"))
                    xfm_done[g] = ev_xsp

                    for sf in stat_fins:
                        sf()
                    stat_fins.clear()
                    erb = p.op("pe", lambda te, g=g: te.matmul(
                        ps_rb[:], ones_f32r[:], r_row[0:1, g * 512:(g + 1) * 512],
                        start=True, stop=True),
                        waits=ev_r_tiles + ps_rb_war, tag="pe")
                    erbc = p.op("dve", lambda v: v.tensor_copy(rbc[:], ps_rb[:]),
                                waits=[erb] + rbc_war, tag="dve")
                    ps_rb_war = [erbc]
                    rbc_war = []

                    k_group_folds = []
                    for kd in range(NKV):
                        bank = kd % 2
                        kps = ps_kv[:, bank * 512:bank * 512 + 512]
                        for d in range(ND):
                            w = ([ev_wk] + ev_xfm_tiles + kv_war[bank]
                                 if d == 0 else [])
                            p.op("pe", lambda te, d=d, kd=kd, kps=kps: te.matmul(
                                kps,
                                wk_sb[:, d * 512 + kd * 128:d * 512 + kd * 128 + 128],
                                xfmg[:, d * 512:(d + 1) * 512],
                                start=(d == 0), stop=False), waits=w)
                            if d == min(8, ND - 1) and \
                                    pending_rope[0] is not None:
                                pending_rope[0]()
                                pending_rope[0] = None
                        efold = p.op("pe", lambda te, kd=kd, g=g, kps=kps:
                                     te.matmul(
                                         kps, s_wk[0:1, kd * 128:kd * 128 + 128],
                                         negmu_bf[0:1, g * 512:(g + 1) * 512],
                                         start=False, stop=True),
                                     waits=[ev_swk_c] + ev_negmu_tiles, tag="pe")
                        k_group_folds.append(efold)

                        def rope_tail(kd=kd, g=g, kps=kps, bank=bank,
                                      efold=efold, erbc=erbc):
                            nonlocal p1_war, p2_war, sqbf_war, ps_nrm_war, \
                                rope_ps_war, nmx_war
                            csg = cossin[:, g * 512:(g + 1) * 512]
                            scg = cossin2[:, g * 512:(g + 1) * 512]
                            kb = kbf[:, (kd % 2) * 512:(kd % 2) * 512 + 512]
                            eP1 = p.op("dve", lambda v, csg=csg, kps=kps:
                                       v.tensor_tensor(
                                           out=p1t[:], in0=kps, in1=csg,
                                           op=ALU.mult),
                                       waits=[efold, ev_tab] + p1_war,
                                       tag="dve")
                            eP2 = p.op("dve", lambda v, scg=scg, kps=kps:
                                       v.tensor_tensor(
                                           out=p2t[:], in0=kps, in1=scg,
                                           op=ALU.mult),
                                       waits=[efold, ev_tab] + p2_war,
                                       tag="dve")
                            kv_war[bank] = [eP1, eP2]
                            eM1 = p.op("pe", lambda te: te.matmul(
                                ps_nrm[:], mt1[:], p1t[:], start=True,
                                stop=False),
                                waits=[eP1] + ev_mt + rope_ps_war)
                            eM2 = p.op("pe", lambda te: te.matmul(
                                ps_nrm[:], mt2[:], p2t[:], start=False,
                                stop=True), waits=[eP2], tag="pe")
                            p1_war = [eM2]
                            p2_war = [eM2]
                            eEv = p.op("dve", lambda v, kb=kb: v.tensor_tensor(
                                out=kb, in0=ps_nrm[:], in1=rbc[:],
                                op=ALU.mult),
                                waits=[eM2, erbc] + kbf_war[kd % 2],
                                tag="dve")
                            rbc_war.append(eEv)
                            en1 = p.op("act", lambda s, kb=kb: s.activation(
                                out=sqbf[:], in_=kb, func=AF.Square),
                                waits=[eEv] + sqbf_war, tag="act")
                            en2 = p.op("pe", lambda te: te.matmul(
                                ps_nrm[0:1, :], ones_bf[:], sqbf[:],
                                start=True, stop=True),
                                waits=[en1, eEv] + ps_nrm_war, tag="pe")
                            sqbf_war = [en2]
                            en3 = p.op("dve", lambda v: v.reduce_max(
                                nmx[0:1, 0:1], ps_nrm[0:1, :], axis=AXX),
                                waits=[en2] + nmx_war, tag="dve")
                            en4 = p.op("dve", lambda v, kd=kd: v.tensor_max(
                                mk_max[0:1, kd:kd + 1], mk_max[0:1, kd:kd + 1],
                                nmx[0:1, 0:1]), waits=[en3], tag="dve")
                            ps_nrm_war = []
                            rope_ps_war = [en3]
                            nmx_war = [en4]
                            esp = p.op("sp", lambda s, kd=kd, g=g, kb=kb:
                                       s.dma_start(
                                           out=kq_s[kd * 128:(kd + 1) * 128,
                                                    g * 512:(g + 1) * 512],
                                           in_=kb), waits=[eEv],
                                       tag=p.q(f"qks{kd % 2}"))
                            kbf_war[kd % 2] = [en1, esp]
                        assert pending_rope[0] is None
                        pending_rope[0] = rope_tail

                    v_group_folds = []
                    for tt in range(4):
                        ti = g * 4 + tt
                        vps = ps_kv[:, 2 * 512:2 * 512 + DKV]
                        for d in range(ND):
                            w = ([ev_wv] + ev_xfm_tiles + kv_war[2]
                                 if d == 0 else [])
                            p.op("pe", lambda te, d=d, tt=tt, vps=vps: te.matmul(
                                vps,
                                xfmg[:, d * 512 + tt * 128:d * 512 + tt * 128 + 128],
                                wv_sb[:, d * 512:d * 512 + DKV],
                                start=(d == 0), stop=False), waits=w)
                            if d == min(8, ND - 1) and \
                                    pending_rope[0] is not None:
                                pending_rope[0]()
                                pending_rope[0] = None
                        efold = p.op("pe", lambda te, ti=ti, vps=vps: te.matmul(
                            vps, negmu_bf[0:1, ti * 128:(ti + 1) * 128],
                            s_wv[0:1, :], start=False, stop=True),
                            waits=[ev_swv_c, ev_negmu_tiles[tt]], tag="pe")
                        v_group_folds.append(efold)
                        vb = vbf[:, (tt % 2) * 512:(tt % 2) * 512 + DKV]
                        eve = p.op("dve", lambda v, ti=ti, vps=vps, vb=vb:
                                   v.tensor_scalar(
                                       out=vb, in0=vps,
                                       scalar1=r_cols[:, ti:ti + 1],
                                       scalar2=None, op0=ALU.mult),
                                   waits=[efold] + vbf_war[tt % 2], tag="dve")
                        kv_war[2] = [eve]
                        evs_ = p.op("sp", lambda s, ti=ti, vb=vb: s.dma_start(
                            out=v_s[ti * 128:(ti + 1) * 128, :], in_=vb),
                            waits=[eve], tag=p.q(f"qvs{tt % 2}"))
                        vbf_war[tt % 2] = [evs_]

                    xfmg_war = [k_group_folds[-1], v_group_folds[-1], ev_xsp]
                p.flush(block)
            p.set_barrier()

            with ExitStack() as _st:
                wq_sb = _st.enter_context(nc.sbuf_tensor("wq_sb", [128, ND * 1024], BF16))
                xfml = _st.enter_context(nc.sbuf_tensor("xfml", [128, 2 * ND * 512], BF16))
                p1q = _st.enter_context(nc.sbuf_tensor("p1q", [128, 512], BF16))
                p2q = _st.enter_context(nc.sbuf_tensor("p2q", [128, 512], BF16))
                qbf = _st.enter_context(nc.sbuf_tensor("qbf", [128, 2 * 512], BF16))
                sqbf2 = _st.enter_context(nc.sbuf_tensor("sqbf2", [128, 512], BF16))
                nmx2 = _st.enter_context(nc.sbuf_tensor("nmx2", [1, 8], F32))
                rbc2 = _st.enter_context(nc.sbuf_tensor("rbc2", [128, 512], F32))
                crow = _st.enter_context(nc.sbuf_tensor("crow", [1, max(NH, 1)], F32))
                ps_q = _st.enter_context(nc.psum_tensor("ps_q", [128, 2 * 512], F32))
                ps_rb2 = _st.enter_context(nc.psum_tensor("ps_rb2", [128, 512], F32))
                ps_nq = _st.enter_context(nc.psum_tensor("ps_nq", [128, 512], F32))
                ps_sq = _st.enter_context(nc.psum_tensor("ps_sq", [128, 512], F32))

                wq_v = wq.rearrange("(d p) w -> p d w", p=128)
                nhps = min(8, NH)
                ev_wq = None
                ev_swq_c = None
                wq_readers = []
                xf_war = {0: [], 1: []}
                q_war = {0: [], 1: []}
                qbf_war = {0: [], 1: []}
                rb2_war = []
                rbc2_war = []
                nq_war = []
                sq_war = []
                nmx2_war = []
                p1q_war = []
                p2q_war = []
                rope_q_war = []
                pending_q = [None]
                ev_mqi2 = p.op("dve", lambda v: v.memset(nmx2[:], 0.0), tag="dve")
                nmx2_war = [ev_mqi2]

                for slab in range(NSLAB):
                    h0 = slab * nhps
                    nh_s = min(nhps, NH - h0)
                    wslab = wq_v[:, :, h0 * 128:(h0 + nh_s) * 128]
                    ev_wq = p.op("pool", lambda g, wslab=wslab, nh_s=nh_s:
                                 g.dma_start(
                                     out=wq_sb[:].rearrange(
                                         "p (d w) -> p d w", w=1024)[
                                         :, :, 0:nh_s * 128],
                                     in_=wslab),
                                 waits=wq_readers, tag=p.q("qwq"))
                    ev_swq_parts = []
                    for hf in range((nh_s * 128 + 511) // 512):
                        w0 = min(512, nh_s * 128 - hf * 512)
                        lastc = None
                        for d in range(ND):
                            w = [ev_wq] + sq_war if d == 0 else []
                            lastc = p.op("pe", lambda te, d=d, hf=hf, w0=w0:
                                         te.matmul(
                                             ps_sq[0:1, 0:w0], ones_bf[:],
                                             wq_sb[:, d * 1024 + hf * 512:
                                                   d * 1024 + hf * 512 + w0],
                                             start=(d == 0),
                                             stop=(d == ND - 1)),
                                         waits=w,
                                         tag="pe" if d == ND - 1 else None)
                        ecp = p.op("dve", lambda v, h0=h0, hf=hf, w0=w0:
                                   v.tensor_copy(
                                       s_wq[0:1, h0 * 128 + hf * 512:
                                            h0 * 128 + hf * 512 + w0],
                                       ps_sq[0:1, 0:w0]),
                                   waits=[lastc], tag="dve")
                        sq_war = [ecp]
                        ev_swq_parts.append(ecp)
                    ev_swq_c = ev_swq_parts[-1]
                    wq_readers = []

                    for g in range(NG):
                        par = g % 2
                        xfv = xfml[:, par * ND * 512:(par + 1) * ND * 512]
                        ev_xl = p.op("sp", lambda s, g=g, xfv=xfv: s.dma_start(
                            out=xfv.rearrange("p (d w) -> p d w", w=512),
                            in_=xfm_s.rearrange("(d p) s -> p d s", p=128)[
                                :, :, g * 512:(g + 1) * 512]),
                            waits=[xfm_done[g]] + xf_war[par], tag=p.q(f"qxl{par}"))
                        erb = p.op("pe", lambda te, g=g: te.matmul(
                            ps_rb2[:], ones_f32r[:],
                            r_row[0:1, g * 512:(g + 1) * 512],
                            start=True, stop=True), waits=rb2_war, tag="pe")
                        erbc = p.op("dve", lambda v: v.tensor_copy(
                            rbc2[:], ps_rb2[:]), waits=[erb] + rbc2_war,
                            tag="dve")
                        rb2_war = [erbc]
                        rbc2_war = []
                        readers = []
                        for hh in range(nh_s):
                            h = h0 + hh
                            bank = hh % 2
                            qps = ps_q[:, bank * 512:bank * 512 + 512]
                            for d in range(ND):
                                w = ([ev_xl, ev_wq] + q_war[bank]
                                     if d == 0 else [])
                                p.op("pe", lambda te, d=d, hh=hh, qps=qps,
                                     xfv=xfv: te.matmul(
                                         qps,
                                         wq_sb[:, d * 1024 + hh * 128:
                                               d * 1024 + hh * 128 + 128],
                                         xfv[:, d * 512:(d + 1) * 512],
                                         start=(d == 0), stop=False), waits=w)
                                if d == min(8, ND - 1) and \
                                        pending_q[0] is not None:
                                    pending_q[0]()
                                    pending_q[0] = None
                            efold = p.op("pe", lambda te, h=h, g=g, qps=qps:
                                         te.matmul(
                                             qps,
                                             s_wq[0:1, h * 128:h * 128 + 128],
                                             negmu_bf[0:1,
                                                      g * 512:(g + 1) * 512],
                                             start=False, stop=True),
                                         waits=[ev_swq_c], tag="pe")
                            wq_readers.append(efold)

                            def q_tail(h=h, g=g, hh=hh, bank=bank, qps=qps,
                                       efold=efold, erbc=erbc,
                                       readers=readers):
                                nonlocal p1q_war, p2q_war, rope_q_war, \
                                    nq_war, nmx2_war
                                csg = cossin[:, g * 512:(g + 1) * 512]
                                scg = cossin2[:, g * 512:(g + 1) * 512]
                                qb = qbf[:, (hh % 2) * 512:(hh % 2) * 512 + 512]
                                eP1 = p.op("dve", lambda v, csg=csg, qps=qps:
                                           v.tensor_tensor(
                                               out=p1q[:], in0=qps, in1=csg,
                                               op=ALU.mult),
                                           waits=[efold, ev_tab] + p1q_war,
                                           tag="dve")
                                eP2 = p.op("dve", lambda v, scg=scg, qps=qps:
                                           v.tensor_tensor(
                                               out=p2q[:], in0=qps, in1=scg,
                                               op=ALU.mult),
                                           waits=[efold, ev_tab] + p2q_war,
                                           tag="dve")
                                q_war[bank] = [eP1, eP2]
                                eM1 = p.op("pe", lambda te: te.matmul(
                                    ps_nq[:], mt1[:], p1q[:], start=True,
                                    stop=False),
                                    waits=[eP1] + ev_mt + rope_q_war)
                                eM2 = p.op("pe", lambda te: te.matmul(
                                    ps_nq[:], mt2[:], p2q[:], start=False,
                                    stop=True), waits=[eP2], tag="pe")
                                p1q_war = [eM2]
                                p2q_war = [eM2]
                                eEv = p.op("dve", lambda v, qb=qb:
                                           v.tensor_tensor(
                                               out=qb, in0=ps_nq[:],
                                               in1=rbc2[:], op=ALU.mult),
                                           waits=[eM2, erbc] + qbf_war[hh % 2],
                                           tag="dve")
                                rbc2_war.append(eEv)
                                en1 = p.op("act", lambda s, qb=qb: s.activation(
                                    out=sqbf2[:], in_=qb, func=AF.Square),
                                    waits=[eEv] + nq_war, tag="act")
                                en2 = p.op("pe", lambda te: te.matmul(
                                    ps_nq[0:1, :], ones_bf[:], sqbf2[:],
                                    start=True, stop=True),
                                    waits=[en1, eEv], tag="pe")
                                en3 = p.op("dve", lambda v: v.reduce_max(
                                    nmx2[0:1, 0:1], ps_nq[0:1, :], axis=AXX),
                                    waits=[en2] + nmx2_war, tag="dve")
                                en4 = p.op("dve", lambda v, h=h: v.tensor_max(
                                    mq_max[0:1, h:h + 1],
                                    mq_max[0:1, h:h + 1],
                                    nmx2[0:1, 0:1]), waits=[en3, ev_mqi],
                                    tag="dve")
                                nq_war = [en2]
                                rope_q_war = [en3]
                                nmx2_war = [en4]
                                esp = p.op("sp", lambda s, h=h, g=g, qb=qb:
                                           s.dma_start(
                                               out=q_s[h * 128:(h + 1) * 128,
                                                       g * 512:(g + 1) * 512],
                                               in_=qb), waits=[eEv],
                                           tag=p.q(f"qqs{hh % 2}"))
                                qbf_war[hh % 2] = [en1, esp]
                                readers.append(esp)
                            assert pending_q[0] is None
                            pending_q[0] = q_tail
                        xf_war[par] = readers

                if pending_q[0] is not None:
                    pending_q[0]()
                    pending_q[0] = None
                ee1 = p.op("dve", lambda v: v.tensor_tensor(
                    out=crow[:].rearrange("o (k g) -> o k g", g=G),
                    in0=mq_max[:].rearrange("o (k g) -> o k g", g=G),
                    in1=mk_max[:].rearrange("o (k g) -> o k g", g=1
                                            ).to_broadcast([1, NKV, G]),
                    op=ALU.mult), waits=nmx2_war + nmx_war, tag="dve")
                ee2 = p.op("act", lambda s: s.activation(
                    out=crow[:], in_=crow[:], func=AF.Sqrt, scale=consts[4]),
                    waits=[ee1], tag="act")
                ee3 = p.op("dve", lambda v: v.tensor_scalar_mul(
                    crow[:], crow[:], -1.0), waits=[ee2], tag="dve")
                ee4 = p.op("pe", lambda te: te.matmul(
                    ps_q[:, 0:NH], ones_f32r[:], crow[:], start=True,
                    stop=True), waits=[ee3] + q_war[0], tag="pe")
                ev_cb = p.op("dve", lambda v: v.tensor_copy(
                    cbias[:], ps_q[:, 0:NH]), waits=[ee4], tag="dve")
                p.flush(block)
        p.set_barrier()

        with ExitStack() as _st:
            arena = _st.enter_context(nc.sbuf_tensor("arena", [128, NH * S], BF16))
            ksb = _st.enter_context(nc.sbuf_tensor("ksb", [128, 2 * S], BF16))
            vsb = _st.enter_context(nc.sbuf_tensor("vsb", [128, 2 * S], BF16))
            qt_sb = _st.enter_context(nc.sbuf_tensor("qt_sb", [128, 4 * 512], BF16))
            pt_sb = _st.enter_context(nc.sbuf_tensor("pt_sb", [128, 8 * 512], BF16))
            masks = _st.enter_context(nc.sbuf_tensor("masks", [128, 4 * 512], F32))
            rden = _st.enter_context(nc.sbuf_tensor("rden", [1, 512], F32))
            rden_b = _st.enter_context(nc.sbuf_tensor("rden_b", [128, 512], F32))
            ps_sc = _st.enter_context(nc.psum_tensor("ps_sc", [128, 3 * 512], F32))
            ps_av = _st.enter_context(nc.psum_tensor("ps_av", [128, 2 * 512], F32))
            ps_dn = _st.enter_context(nc.psum_tensor("ps_dn", [1, 2 * 512], F32))
            ps_rb3 = _st.enter_context(nc.psum_tensor("ps_rb3", [128, 512], F32))

            ev_msk = []
            for c in range(4):
                m = masks[:, c * 512:(c + 1) * 512]
                em1 = p.op("pool", lambda g, m=m: g.memset(m, 0.0), tag="pool")
                em2 = p.op("pool", lambda g, m=m, c=c: g.affine_select(
                    out=m, in_=m, compare_op=ALU.is_ge, fill=MASKVAL,
                    base=-(c * 128), pattern=[[1, 512]], channel_multiplier=-1),
                    waits=[em1], tag="pool")
                ev_msk.append(em2)

            k_war = {0: [], 1: []}
            v_war = {0: [], 1: []}
            q2_war = {0: [], 1: [], 2: [], 3: []}
            pt_war = [None] * 8
            sc_war = [[], [], []]
            av_war = [[], []]
            dnp_war = {0: [], 1: []}
            rb3_war = []
            rden_war = []
            rdenb_war = []
            arena_evs = []
            pending_fin = [None]

            v_v = v_s.rearrange("(c p) w -> p c w", p=128)
            groups = [[2 * b, 2 * b + 1] for b in range(cfg["B"])]
            NH2 = NH // 2
            cc_evs = []

            for hk in range(NKV):
                if hk == NKV // 2 and NKV > 1:
                    if pending_fin[0] is not None:
                        pending_fin[0]()
                        pending_fin[0] = None
                    ev_send1 = p.op("sp", lambda s: s.dma_start(
                        out=send_s[0:NH2 * 128, :].rearrange(
                            "(h p) s -> p h s", p=128),
                        in_=arena[:, 0:NH2 * S].rearrange(
                            "p (h s) -> p h s", s=S)),
                        waits=list(arena_evs), tag=p.q("qsend"))
                    cc_evs.append(p.op("pool", lambda g: g.collective_compute(
                        "AllGather", ALU.bypass, replica_groups=groups,
                        ins=[send_s[0:NH2 * 128, :].opt()],
                        outs=[recv_s[0:2 * NH2 * 128, :].opt()]),
                        waits=[ev_send1], tag=p.q1("qcc")))
                kpar = hk % 2
                ksl = ksb[:, kpar * S:(kpar + 1) * S]
                vsl = vsb[:, kpar * S:(kpar + 1) * S]
                ev_k = p.op("sp", lambda s, hk=hk, ksl=ksl: s.dma_start(
                    out=ksl, in_=kq_s[hk * 128:(hk + 1) * 128, :]),
                    waits=k_war[kpar], tag=p.q(f"qk{kpar}"))
                ev_v = p.op("sp", lambda s, hk=hk, vsl=vsl: s.dma_start(
                    out=vsl.rearrange("p (c w) -> p c w", w=128),
                    in_=v_v[:, :, hk * 128:(hk + 1) * 128]),
                    waits=v_war[kpar], tag=p.q(f"qv{kpar}"))
                k_readers = []
                v_readers = []
                for gq in range(G):
                    h = hk * G + gq
                    for J in range(NJ):
                        qpar = (hk * G * NJ + gq * NJ + J) % 4
                        qv = qt_sb[:, qpar * 512:(qpar + 1) * 512]
                        ev_q = p.op("sp", lambda s, h=h, J=J, qv=qv: s.dma_start(
                            out=qv, in_=q_s[h * 128:(h + 1) * 128,
                                            J * 512:(J + 1) * 512]),
                            waits=q2_war[qpar], tag=p.q(f"qq{qpar}"))
                        nch = 4 * (J + 1)
                        avb = (h * NJ + J) % 2
                        avp = ps_av[:, avb * 512:avb * 512 + 512]
                        dnp = ps_dn[0:1, avb * 512:avb * 512 + 512]
                        ev_avmm = None
                        ev_dnmm = None
                        last_exp = None
                        exp_evs = [None] * nch
                        SKEW = 2
                        for cc_i in range(nch + SKEW):
                            if cc_i < nch:
                                c = cc_i
                                scb = c % 3
                                scp = ps_sc[:, scb * 512:scb * 512 + 512]
                                w = [ev_k, ev_q] + sc_war[scb]
                                emm = p.op("pe", lambda te, c=c, scp=scp,
                                           ksl=ksl, qv=qv: te.matmul(
                                               scp,
                                               ksl[:, c * 128:(c + 1) * 128],
                                               qv, start=True, stop=True),
                                           waits=w, tag="pe")
                                wexp = [emm]
                                if c >= nch - 4:
                                    mcc = c - (nch - 4)
                                    emsk = p.op("dve", lambda v, scp=scp,
                                                mcc=mcc: v.tensor_tensor(
                                                    out=scp, in0=scp,
                                                    in1=masks[:, mcc * 512:(mcc + 1) * 512],
                                                    op=ALU.add),
                                                waits=[emm, ev_msk[mcc]],
                                                tag="dve")
                                    wexp = [emsk]
                                ptv = pt_sb[:, (c % 8) * 512:(c % 8) * 512 + 512]
                                wpt = [pt_war[c % 8]] if pt_war[c % 8] else []
                                eexp = p.op("act", lambda s, scp=scp, ptv=ptv,
                                            h=h: s.activation(
                                                out=ptv, in_=scp, func=AF.Exp,
                                                scale=consts[5],
                                                bias=cbias[:, h:h + 1]),
                                            waits=wexp + wpt + [ev_cb],
                                            tag="act")
                                exp_evs[c] = eexp
                                last_exp = eexp
                                sc_war[scb] = [eexp]
                            if cc_i >= SKEW:
                                c = cc_i - SKEW
                                ptv = pt_sb[:, (c % 8) * 512:(c % 8) * 512 + 512]
                                w_av = [exp_evs[c], ev_v]
                                if c == 0:
                                    w_av += av_war[avb]
                                ev_avmm = p.op("pe", lambda te, c=c, avp=avp,
                                               vsl=vsl, ptv=ptv, nch=nch:
                                               te.matmul(
                                                   avp,
                                                   vsl[:, c * 128:(c + 1) * 128],
                                                   ptv, start=(c == 0),
                                                   stop=(c == nch - 1)),
                                               waits=w_av,
                                               tag="pe" if c == nch - 1
                                               else None)
                                w_dn = [exp_evs[c]]
                                if c == 0:
                                    w_dn += dnp_war[avb]
                                ev_dnmm = p.op("pe", lambda te, c=c, ptv=ptv,
                                               dnp=dnp, nch=nch: te.matmul(
                                                   dnp, ones_bf[:], ptv,
                                                   start=(c == 0),
                                                   stop=(c == nch - 1)),
                                               waits=w_dn, tag="pe")
                                pt_war[c % 8] = ev_dnmm
                            if cc_i == 2 and pending_fin[0] is not None:
                                pending_fin[0]()
                                pending_fin[0] = None
                        k_readers.append(last_exp)
                        v_readers.append(ev_avmm)
                        q2_war[qpar] = [last_exp]

                        def fin(ev_avmm=ev_avmm, ev_dnmm=ev_dnmm, avp=avp,
                                dnp=dnp, avb=avb, h=h, J=J,
                                k_readers=k_readers, v_readers=v_readers):
                            nonlocal rden_war, rb3_war, rdenb_war
                            ed1 = p.op("dve", lambda v, dnp=dnp: v.reciprocal(
                                rden[:], dnp), waits=[ev_dnmm] + rden_war,
                                tag="dve")
                            dnp_war[avb] = [ed1]
                            ed2 = p.op("pe", lambda te: te.matmul(
                                ps_rb3[:], ones_f32r[:], rden[:],
                                start=True, stop=True),
                                waits=[ed1] + rb3_war, tag="pe")
                            rden_war = [ed2]
                            ed3 = p.op("dve", lambda v: v.tensor_copy(
                                rden_b[:], ps_rb3[:]),
                                waits=[ed2] + rdenb_war, tag="dve")
                            rb3_war = [ed3]
                            dst = arena[:, h * S + J * 512:
                                        h * S + (J + 1) * 512]
                            ee = p.op("dve", lambda v, avp=avp, dst=dst:
                                      v.tensor_tensor(out=dst, in0=avp,
                                                      in1=rden_b[:],
                                                      op=ALU.mult),
                                      waits=[ev_avmm, ed3], tag="dve")
                            av_war[avb] = [ee]
                            rdenb_war = [ee]
                            arena_evs.append(ee)
                        assert pending_fin[0] is None
                        pending_fin[0] = fin
                k_war[kpar] = k_readers
                v_war[kpar] = v_readers
            pending_fin[0]()
            pending_fin[0] = None

            lo = NH2 * 128 if NKV > 1 else 0
            ev_send = p.op("sp", lambda s, lo=lo: s.dma_start(
                out=send_s[lo:NH * 128, :].rearrange("(h p) s -> p h s", p=128),
                in_=arena[:, lo * S // 128:NH * S].rearrange(
                    "p (h s) -> p h s", s=S)),
                waits=list(arena_evs), tag=p.q("qsend"))
            cc_evs.append(p.op("pool", lambda g, lo=lo: g.collective_compute(
                "AllGather", ALU.bypass, replica_groups=groups,
                ins=[send_s[lo:NH * 128, :].opt()],
                outs=[recv_s[2 * lo:2 * NH * 128, :].opt()]),
                waits=[ev_send], tag=p.q1("qcc")))
            p.flush(block)
        p.set_barrier()

        if NKV > 1:
            perm = (list(range(0, NH2)) + list(range(NH, NH + NH2))
                    + list(range(NH2, NH)) + list(range(NH + NH2, 2 * NH)))
        else:
            perm = list(range(2 * NH))
        _emit_back(nc, p, block, cfg, consts, t, cc_evs, recv_t,
                   x_my, wo, rwT, rb, eb, y, x1_s, perm)

    return nc



def _emit_back(nc, p, block, cfg, consts, t, cc_evs, recv_t,
               x_my, wo, rwT, rb, eb, y, x1_s, perm):
    from contextlib import ExitStack
    S, D, ND, NH = (cfg[k] for k in ("S", "D", "ND", "NH"))
    SQ2, NT2, NDO = (cfg[k] for k in ("SQ2", "NT2", "NDO"))
    H = cfg["H"]
    E = E_EXPERTS
    invD = 1.0 / D
    if True:
        id_bf, id_f32 = t["id_bf"][:], t["id_f32"][:]
        ones_bf, ones_f32r = t["ones_bf"][:], t["ones_f32r"][:]
        ev_setup = t["ev_setup"]

        with ExitStack() as _st:
            arena = _st.enter_context(nc.sbuf_tensor("arena3", [128, H * SQ2], BF16))
            wo_sb = _st.enter_context(nc.sbuf_tensor("wo_sb", [128, 2 * ND * 512], BF16))
            xcol = _st.enter_context(nc.sbuf_tensor("xcol", [128, 3 * 512], F32))
            x1col = _st.enter_context(nc.sbuf_tensor("x1col", [128, 3 * 512], F32))
            ps_o = _st.enter_context(nc.psum_tensor("ps_o", [128, 2 * 512], F32))

            pid_sb = _st.enter_context(
                nc.sbuf_tensor("pid_sb", [1, 1], mybir.dt.uint32))
            rg = _st.enter_context(nc.gpsimd.register("rg_half"))
            ev_pid = p.op("pool", lambda g: g.dma_start(
                out=pid_sb[0:1, 0:1], in_=nc.partition_id_tensor.ap()),
                tag=p.q("qpid"))
            p.op("pool", lambda g: g.reg_load(rg, pid_sb[0:1, 0:1]),
                 waits=[ev_pid])
            p.op("pool", lambda g: g.reg_mod(rg, rg, 2))
            p.op("pool", lambda g: g.reg_mul(rg, rg, SQ2))
            ev_ar = p.op("pool", lambda g: g.dma_start(
                out=arena[:].rearrange("p (h s) -> p h s", s=SQ2),
                in_=bass.AP(recv_t, rg, [[S, 128], [128 * S, H], [1, SQ2]])),
                waits=list(cc_evs), tag=p.q("qar"))

            wo_v = wo.rearrange("(d p) c -> p d c", p=128)
            wo_war = {0: [], 1: []}
            xc_war = {0: [], 1: [], 2: []}
            x1_war = {0: [], 1: [], 2: []}
            o_war = {0: [], 1: []}
            for do in range(NDO):
                wpar = do % 2
                wsl = wo_sb[:, wpar * ND * 512:(wpar + 1) * ND * 512]
                ev_w = p.op("pool", lambda g, do=do, wsl=wsl: g.dma_start(
                    out=wsl.rearrange("p (d w) -> p d w", w=512),
                    in_=wo_v[:, :, do * 512:(do + 1) * 512]),
                    waits=wo_war[wpar] + ev_setup, tag=p.q(f"qwo{wpar}"))
                w_readers = []
                for tt in range(NT2):
                    par = (do * NT2 + tt) % 3
                    ob = (do * NT2 + tt) % 2
                    op_ = ps_o[:, ob * 512:ob * 512 + 512]
                    xcv = xcol[:, par * 512:(par + 1) * 512]
                    x1v = x1col[:, par * 512:(par + 1) * 512]
                    ev_xc = p.op("sp", lambda s, tt=tt, do=do, xcv=xcv:
                                 s.dma_start(
                                     out=xcv,
                                     in_=x_my[tt * 128:(tt + 1) * 128,
                                              do * 512:(do + 1) * 512]),
                                 waits=xc_war[par], tag=p.q(f"qxc{par}"))
                    last = None
                    for hc in range(H):
                        w = [ev_ar, ev_w] + o_war[ob] if hc == 0 else []
                        last = p.op("pe", lambda te, hc=hc, tt=tt, op_=op_,
                                    wsl=wsl: te.matmul(
                                        op_,
                                        arena[:, hc * SQ2 + tt * 128:
                                              hc * SQ2 + tt * 128 + 128],
                                        wsl[:, perm[hc] * 512:
                                            perm[hc] * 512 + 512],
                                        start=(hc == 0), stop=(hc == H - 1)),
                                    waits=w, tag="pe" if hc == H - 1 else None)
                    ee = p.op("dve", lambda v, op_=op_, xcv=xcv, x1v=x1v:
                              v.tensor_tensor(out=x1v, in0=op_, in1=xcv,
                                              op=ALU.add),
                              waits=[last, ev_xc] + x1_war[par], tag="dve")
                    o_war[ob] = [ee]
                    es = p.op("sp", lambda s, tt=tt, do=do, x1v=x1v: s.dma_start(
                        out=x1_s[tt * 128:(tt + 1) * 128,
                                 do * 512:(do + 1) * 512],
                        in_=x1v), waits=[ee], tag=p.q(f"qx1{par}"))
                    x1_war[par] = [es]
                    xc_war[par] = [ee]
                    w_readers.append(last)
                wo_war[wpar] = w_readers
            p.flush(block)
        p.set_barrier()

        with ExitStack() as _st:
            x1t = _st.enter_context(nc.sbuf_tensor("x1t", [128, 2 * D], F32))
            x1bf = _st.enter_context(nc.sbuf_tensor("x1bf", [128, D], BF16))
            x1fm = _st.enter_context(nc.sbuf_tensor("x1fm", [128, ND * 128], BF16))
            rw_sb = _st.enter_context(nc.sbuf_tensor("rw_sb", [128, ND * 64], BF16))
            eb_sb = _st.enter_context(nc.sbuf_tensor("eb_sb", [64, D], BF16))
            rb_sb = _st.enter_context(nc.sbuf_tensor("rb_sb", [1, E], F32))
            rb_bc = _st.enter_context(nc.sbuf_tensor("rb_bc", [128, E], F32))
            s_rw = _st.enter_context(nc.sbuf_tensor("s_rw", [1, E], BF16))
            negmu2_bf = _st.enter_context(nc.sbuf_tensor("negmu2_bf", [1, 2 * 128], BF16))
            stat2 = _st.enter_context(nc.sbuf_tensor("stat2", [128, 16], F32))
            trash2 = _st.enter_context(nc.sbuf_tensor("trash2", [128, D], BF16))
            lgs = _st.enter_context(nc.sbuf_tensor("lgs", [128, E], F32))
            pex = _st.enter_context(nc.sbuf_tensor("pex", [128, E], F32))
            den = _st.enter_context(nc.sbuf_tensor("den", [128, 4], F32))
            t8 = _st.enter_context(nc.sbuf_tensor("t8", [128, 8], F32))
            repl = _st.enter_context(nc.sbuf_tensor("repl", [128, E], F32))
            mskb = _st.enter_context(nc.sbuf_tensor("mskb", [128, E], BF16))
            wdT = _st.enter_context(nc.sbuf_tensor("wdT", [64, 2 * 128], BF16))
            ln2 = _st.enter_context(nc.sbuf_tensor("ln2", [128, D], F32))
            t2 = _st.enter_context(nc.sbuf_tensor("t2", [128, D], F32))
            outt = _st.enter_context(nc.sbuf_tensor("outt", [128, 2 * D], F32))
            ps_tr2 = _st.enter_context(nc.psum_tensor("ps_tr2", [128, 1024], BF16))
            ps_r = _st.enter_context(nc.psum_tensor("ps_r", [128, 512], F32))
            ps_b = _st.enter_context(nc.psum_tensor("ps_b", [128, 2 * 512], F32))
            ps_s2 = _st.enter_context(nc.psum_tensor("ps_s2", [128, 512], F32))
            ps_wd = _st.enter_context(nc.psum_tensor("ps_wd", [64, 2 * 128], BF16))

            rw_v = rwT.rearrange("(d p) e -> p d e", p=128)
            ev_rw = p.op("pool", lambda g: g.dma_start(
                out=rw_sb[:].rearrange("p (d e) -> p d e", e=E), in_=rw_v),
                tag=p.q("qrw"))
            ev_eb = p.op("pool", lambda g: g.dma_start(
                out=eb_sb[:], in_=eb), tag=p.q("qeb"))
            ev_rb = p.op("sp", lambda s: s.dma_start(out=rb_sb[:], in_=rb),
                         tag=p.q("qrb"))
            lastc = None
            for d in range(ND):
                w = [ev_rw] + ev_setup if d == 0 else []
                lastc = p.op("pe", lambda te, d=d: te.matmul(
                    ps_s2[0:1, 0:E], ones_bf[:], rw_sb[:, d * E:(d + 1) * E],
                    start=(d == 0), stop=(d == ND - 1)),
                    waits=w, tag="pe" if d == ND - 1 else None)
            ev_srw = p.op("dve", lambda v: v.tensor_copy(
                s_rw[:], ps_s2[0:1, 0:E]), waits=[lastc], tag="dve")
            eb1 = p.op("pe", lambda te: te.matmul(
                ps_r[:, 0:E], ones_f32r[:], rb_sb[:], start=True, stop=True),
                waits=[ev_rb] + ev_setup, tag="pe")
            ev_rbc = p.op("dve", lambda v: v.tensor_copy(
                rb_bc[:], ps_r[:, 0:E]), waits=[eb1], tag="dve")

            x1_war = {0: [], 1: []}
            tr2_war = [None]
            st_war = [ev_srw]
            r_war = [ev_rbc]
            lgs_war = []
            pex_war = []
            repl_war = []
            mskb_war = []
            t8_war = []
            den_war = {0: [], 1: []}
            rd_war = {0: [], 1: []}
            wd_war = {0: [], 1: []}
            pwd_war = {0: [], 1: []}
            b_war = {0: [], 1: []}
            out_war = {0: [], 1: []}
            ln2_war = []
            t2_war = []
            x1bf_war = []
            x1fm_war = []
            trash2_war = []
            mx_war = {0: [], 1: []}

            for tt in range(NT2):
                par = tt % 2
                xv = x1t[:, par * D:(par + 1) * D]
                ev_l = p.op("sp", lambda s, tt=tt, xv=xv: s.dma_start(
                    out=xv, in_=x1_s[tt * 128:(tt + 1) * 128, :]),
                    waits=x1_war[par], tag=p.q(f"qxt{par}"))
                c_sum = stat2[:, par * 8 + 0: par * 8 + 1]
                c_sq = stat2[:, par * 8 + 1: par * 8 + 2]
                c_negmu = stat2[:, par * 8 + 2: par * 8 + 3]
                c_mu2 = stat2[:, par * 8 + 3: par * 8 + 4]
                c_var = stat2[:, par * 8 + 4: par * 8 + 5]
                c_sd = stat2[:, par * 8 + 5: par * 8 + 6]
                c_r = stat2[:, par * 8 + 6: par * 8 + 7]
                c_mx = stat2[:, par * 8 + 7: par * 8 + 8]
                e1 = p.op("dve", lambda v, a=c_sum, b=xv: v.reduce_sum(
                    a, b, axis=AXX), waits=[ev_l], tag="dve")
                e2 = p.op("act", lambda s, a=c_sq, b=xv: s.activation(
                    out=trash2[:], in_=b, func=AF.Square, accum_out=a),
                    waits=[ev_l] + trash2_war, tag="act")
                trash2_war = [e2]
                e3 = p.op("dve", lambda v, a=c_negmu, b=c_sum:
                          v.tensor_scalar_mul(a, b, -invD), waits=[e1],
                          tag="dve")
                e4 = p.op("dve", lambda v, a=c_mu2, b=c_negmu: v.tensor_mul(
                    a, b, b), waits=[e3], tag="dve")
                e5 = p.op("dve", lambda v, a=c_var, b=c_sq, c=c_mu2:
                          v.scalar_tensor_tensor(out=a, in0=b, scalar=invD,
                                                 in1=c, op0=ALU.mult,
                                                 op1=ALU.subtract),
                          waits=[e2, e4], tag="dve")
                e6 = p.op("act", lambda s, a=c_sd, b=c_var: s.activation(
                    out=a, in_=b, func=AF.Sqrt, bias=consts[3]),
                    waits=[e5], tag="act")
                e7 = p.op("dve", lambda v, a=c_r, b=c_sd: v.reciprocal(a, b),
                          waits=[e6], tag="dve")
                e8 = p.op("pe", lambda te, b=c_negmu: te.transpose(
                    ps_s2[0:1, 0:128], b, id_f32), waits=[e3] + st_war,
                    tag="pe")
                e9 = p.op("dve", lambda v, par=par: v.tensor_copy(
                    negmu2_bf[0:1, par * 128:(par + 1) * 128],
                    ps_s2[0:1, 0:128]), waits=[e8], tag="dve")
                st_war = [e9]
                e10 = p.op("dve", lambda v, xv=xv: v.tensor_copy(x1bf[:], xv),
                           waits=[ev_l] + x1bf_war, tag="dve")
                tr_evs = []
                x1bf_war = []
                NB8 = min(8, ND)
                for bk in range(ND // NB8):
                    bank = ps_tr2[:, 0:NB8 * 128]
                    lt = None
                    for dd in range(NB8):
                        d = bk * NB8 + dd
                        w = [e10]
                        if dd == 0 and tr2_war[0] is not None:
                            w.append(tr2_war[0])
                        lt = p.op("pe", lambda te, d=d, dd=dd, bank=bank:
                                  te.transpose(
                                      bank[:, dd * 128:(dd + 1) * 128],
                                      x1bf[:, d * 128:(d + 1) * 128], id_bf),
                                  waits=w, tag="pe" if dd == NB8 - 1 else None)
                    dst = x1fm[:].rearrange("p (d w) -> p d w", w=128)[
                        :, bk * NB8:(bk + 1) * NB8, :]
                    eev = p.op("act", lambda s, bank=bank, dst=dst: s.copy(
                        out=dst, in_=bank.rearrange("p (d w) -> p d w", w=128)),
                        waits=[lt] + x1fm_war, tag="act")
                    tr2_war[0] = eev
                    tr_evs.append(eev)
                    x1bf_war.append(lt)
                lastr = None
                for d in range(ND):
                    w = tr_evs + [ev_rw] + r_war if d == 0 else []
                    lastr = p.op("pe", lambda te, d=d: te.matmul(
                        ps_r[:, 0:E], x1fm[:, d * 128:(d + 1) * 128],
                        rw_sb[:, d * E:(d + 1) * E],
                        start=(d == 0), stop=False), waits=w)
                ef = p.op("pe", lambda te, par=par: te.matmul(
                    ps_r[:, 0:E], negmu2_bf[0:1, par * 128:(par + 1) * 128],
                    s_rw[0:1, :], start=False, stop=True),
                    waits=[ev_srw, e9], tag="pe")
                x1fm_war = [ef]
                eg = p.op("dve", lambda v, c=c_r: v.scalar_tensor_tensor(
                    out=lgs[:], in0=ps_r[:, 0:E], scalar=c, in1=rb_bc[:],
                    op0=ALU.mult, op1=ALU.add),
                    waits=[ef, e7, ev_rbc] + lgs_war, tag="dve")
                r_war = [eg]
                eh1 = p.op("dve", lambda v, a=c_mx: v.reduce_max(
                    a, lgs[:], axis=AXX, negate=True),
                    waits=[eg] + mx_war[par], tag="dve")
                eh2 = p.op("act", lambda s, a=c_mx, dpar=par: s.activation(
                    out=pex[:], in_=lgs[:], func=AF.Exp, bias=a,
                    accum_out=den[:, dpar:dpar + 1]),
                    waits=[eh1] + pex_war + den_war[par], tag="act")
                mx_war[par] = [eh2]
                eh3 = p.op("dve", lambda v: v.max(out=t8[:], in_=pex[:]),
                           waits=[eh2] + t8_war, tag="dve")
                eh4 = p.op("dve", lambda v: v.match_replace(
                    out=repl[:], in_to_replace=t8[:], in_values=pex[:],
                    imm_value=0.0), waits=[eh3] + repl_war, tag="dve")
                c_w8 = stat2[:, par * 8 + 3: par * 8 + 4]
                eh5 = p.op("dve", lambda v, a=c_w8: v.reduce_sum(
                    a, t8[:], axis=AXX), waits=[eh4, e5], tag="dve")
                eh6 = p.op("dve", lambda v: v.tensor_sub(
                    mskb[:], pex[:], repl[:]), waits=[eh4] + mskb_war,
                    tag="dve")
                lgs_war = [eh1, eh2]
                pex_war = [eh3, eh4, eh6]
                repl_war = [eh6]
                t8_war = [eh4, eh5]
                wdv = wdT[:, par * 128:(par + 1) * 128]
                pwd = ps_wd[:, par * 128:(par + 1) * 128]
                eh7 = p.op("pe", lambda te, pwd=pwd: te.transpose(
                    pwd, mskb[:], id_bf),
                    waits=[eh6] + wd_war[par] + pwd_war[par], tag="pe")
                mskb_war = [eh7]
                eh8 = p.op("dve", lambda v, pwd=pwd, wdv=wdv: v.tensor_copy(
                    wdv, pwd), waits=[eh7], tag="dve")
                pwd_war[par] = [eh8]
                c_rd = stat2[:, par * 8 + 4: par * 8 + 5]
                ei1 = p.op("dve", lambda v, a=c_rd, dpar=par: v.reciprocal(
                    a, den[:, dpar:dpar + 1]),
                    waits=[eh2, e6] + rd_war[par], tag="dve")
                c_wn = stat2[:, par * 8 + 0: par * 8 + 1]
                ei2 = p.op("dve", lambda v, a=c_wn, b=c_w8, c=c_rd:
                           v.tensor_mul(a, b, c),
                           waits=[eh5, ei1, e3], tag="dve")
                ej1 = p.op("dve", lambda v, a=c_negmu, b=c_r, xv=xv:
                           v.tensor_scalar(out=ln2[:], in0=xv, scalar1=a,
                                           scalar2=b, op0=ALU.add,
                                           op1=ALU.mult),
                           waits=[e7, eg] + ln2_war + out_war[par], tag="dve")
                ej2 = p.op("dve", lambda v, a=c_wn, xv=xv:
                           v.scalar_tensor_tensor(out=t2[:], in0=ln2[:],
                                                  scalar=a, in1=xv,
                                                  op0=ALU.mult, op1=ALU.add),
                           waits=[ej1, ei2] + t2_war, tag="dve")
                ln2_war = [ej2]
                ov = outt[:, par * D:(par + 1) * D]
                out_evs = []
                for do in range(NDO):
                    bb = (tt * NDO + do) % 2
                    pb = ps_b[:, bb * 512:bb * 512 + 512]
                    w = [eh8, ev_eb] + b_war[bb]
                    em = p.op("pe", lambda te, do=do, pb=pb, wdv=wdv: te.matmul(
                        pb, wdv, eb_sb[:, do * 512:(do + 1) * 512],
                        start=True, stop=True), waits=w, tag="pe")
                    eo = p.op("dve", lambda v, do=do, pb=pb, ov=ov, c=c_rd:
                              v.scalar_tensor_tensor(
                                  out=ov[:, do * 512:(do + 1) * 512], in0=pb,
                                  scalar=c,
                                  in1=t2[:, do * 512:(do + 1) * 512],
                                  op0=ALU.mult, op1=ALU.add),
                              waits=[em, ej2, ei1], tag="dve")
                    b_war[bb] = [eo]
                    out_evs.append(eo)
                t2_war = out_evs
                wd_war[par] = [em]
                den_war[par] = [ei1, ei2]
                es = p.op("sp", lambda s, tt=tt, ov=ov: s.dma_start(
                    out=y[tt * 128:(tt + 1) * 128, :], in_=ov),
                    waits=out_evs, tag=p.q(f"qy{par}"))
                out_war[par] = [es]
                rd_war[par] = out_evs
                x1_war[par] = [e1, e2, e5, e10, ej1, ej2]

            p.op("pool", lambda g: g.engine_nop(),
                 waits=[("qy0", p.count["qy0"]),
                        ("qy1", p.count.get("qy1", 0))]
                 if p.count.get("qy1", 0) else [("qy0", p.count["qy0"])],
                 tag="pool")
            p.flush(block)



def _run(cfg, inputs, n_cores=8, runner=None):
    cfg_d = _derive(dict(cfg, B=n_cores // 2))
    S, D = cfg["S"], cfg["D"]
    NH, NKV, SQ2 = cfg_d["NH"], cfg_d["NKV"], cfg_d["SQ2"]
    X = np.ascontiguousarray(inputs["X"], dtype=np.float32)
    Wq = np.ascontiguousarray(inputs["Wq"], dtype=np.float32)
    Wk = np.ascontiguousarray(inputs["Wk"], dtype=np.float32)
    Wv = np.ascontiguousarray(inputs["Wv"], dtype=np.float32)
    Wo = np.ascontiguousarray(inputs["Wo"], dtype=np.float32)
    RW = np.ascontiguousarray(inputs["RW"], dtype=np.float32)
    RB = np.ascontiguousarray(inputs["RB"], dtype=np.float32).reshape(1, -1)
    EB = np.ascontiguousarray(inputs["EB"], dtype=np.float32)
    B = X.shape[0]
    assert n_cores == 2 * B
    if runner is None:
        runner = lambda nc_, maps: run_bass_kernel_spmd(
            nc_, maps, core_ids=list(range(n_cores))).results

    nc = build_fused(dict(cfg, B=B))
    rwT = np.ascontiguousarray(RW.T)
    in_maps = []
    for c in range(n_cores):
        b, half = c // 2, c % 2
        in_maps.append({
            "x_in": X[b],
            "wq": np.ascontiguousarray(
                Wq[:, half * NH * 128:(half + 1) * NH * 128]),
            "wk": np.ascontiguousarray(
                Wk[:, half * NKV * 128:(half + 1) * NKV * 128]),
            "wv": np.ascontiguousarray(
                Wv[:, half * NKV * 128:(half + 1) * NKV * 128]),
            "x_my": np.ascontiguousarray(X[b, half * SQ2:(half + 1) * SQ2, :]),
            "wo": Wo,
            "rwT": rwT,
            "rb": RB,
            "eb": EB,
        })
    res = runner(nc, in_maps)
    ys = [r["y"] for r in res]
    out = np.empty((B, S, D), dtype=np.float32)
    for c in range(n_cores):
        b, half = c // 2, c % 2
        out[b, half * SQ2:(half + 1) * SQ2, :] = ys[c]
    return out


def kernel(L0_attn_X, L0_attn_Wq, L0_attn_Wk, L0_attn_Wv, L0_attn_Wo,
           L0_ffn_router_weight, L0_ffn_router_bias, L0_ffn_expert_bias):
    inputs = dict(X=L0_attn_X, Wq=L0_attn_Wq, Wk=L0_attn_Wk, Wv=L0_attn_Wv,
                  Wo=L0_attn_Wo, RW=L0_ffn_router_weight,
                  RB=L0_ffn_router_bias, EB=L0_ffn_expert_bias)
    return _run(REAL, inputs, n_cores=8)


# version 23 (final)
# speedup vs baseline: 1.0336x; 1.0336x over previous
import numpy as np
import ml_dtypes
import concourse.bass as bass
import concourse.mybir as mybir
from concourse.bass_utils import run_bass_kernel_spmd

F32 = mybir.dt.float32
BF16 = mybir.dt.bfloat16
I32 = mybir.dt.int32
AF = mybir.ActivationFunctionType
ALU = mybir.AluOpType
AXX = mybir.AxisListType.X

REAL = dict(S=4096, D=4096, H=32, HKV=8)
E_EXPERTS = 64
EPS = 1e-5
THETA = 10000.0
MASKVAL = -30000.0
CSAFE = 1.05

ENG2BLOCK = {"pe": "tensor", "dve": "vector", "act": "scalar",
             "pool": "gpsimd", "sp": "sync"}


class Prog:

    def __init__(self, nc):
        self.nc = nc
        self.sem_ctx = []
        self.sems = {}
        self.count = {}
        for q, step in (("pe", 1), ("dve", 1), ("act", 1), ("pool", 1)):
            self._mk_queue(q, step)
        self.items = {e: [] for e in ENG2BLOCK}
        self.watermark = {e: {} for e in ENG2BLOCK}
        self.pending_barrier = {e: [] for e in ENG2BLOCK}

    def _mk_queue(self, q, step):
        guard = self.nc.semaphore(f"sem_{q}")
        sem = guard.__enter__()
        self.sem_ctx.append(guard)
        self.sems[q] = (sem, step)
        if not hasattr(self, "count") or self.count is None:
            self.count = {}
        self.count[q] = 0

    def q(self, name):
        if name not in self.sems:
            self._mk_queue(name, 16)
        return name

    def q1(self, name):
        if name not in self.sems:
            self._mk_queue(name, 1)
        return name

    def set_barrier(self, evs=None):
        evs = list(evs) if evs is not None else self.barrier_events()
        for e in ENG2BLOCK:
            self.pending_barrier[e] = evs

    def op(self, stream, fn, waits=(), tag=None):
        waits = list(waits)
        if self.pending_barrier[stream]:
            waits = self.pending_barrier[stream] + waits
            self.pending_barrier[stream] = []
        for w in waits:
            assert w is not None, "wait on untagged producer"
        ev = None
        if tag is not None:
            self.count[tag] += 1
            ev = (tag, self.count[tag])
        self.items[stream].append((fn, tuple(waits), tag))
        return ev

    def flush(self, block):
        for e in ENG2BLOCK:
            items = self.items[e]
            if not items:
                continue
            self.items[e] = []

            def body(eng, e=e, items=items):
                wm = self.watermark[e]
                for fn, waits, tagq in items:
                    for (q, v) in waits:
                        if v > wm.get(q, 0):
                            sem, step = self.sems[q]
                            eng.wait_ge(sem, v * step)
                            wm[q] = v
                    instr = fn(eng)
                    if tagq is not None:
                        sem, step = self.sems[tagq]
                        instr.then_inc(sem, step)
            getattr(block, ENG2BLOCK[e])(body)

    def barrier_events(self):
        return [(q, c) for q, c in self.count.items() if c > 0]


def _alloc(st, nc, name, shape, dtype, psum=False):
    f = nc.psum_tensor if psum else nc.sbuf_tensor
    return st.enter_context(f(name, shape, dtype))


def _derive(cfg):
    S, D, H, HKV = cfg["S"], cfg["D"], cfg["H"], cfg["HKV"]
    d = dict(cfg)
    d["ND"] = D // 128
    d["NH"] = H // 2
    d["NKV"] = HKV // 2
    d["G"] = H // HKV
    d["SQ2"] = S // 2
    d["NG"] = S // 512
    d["NT"] = S // 128
    d["NT2"] = d["SQ2"] // 128
    d["NDO"] = D // 512
    d["NC"] = S // 128
    d["NJ"] = S // 512
    d["NSLAB"] = (d["NH"] + 7) // 8
    d.setdefault("B", 4)
    return d


def _setup_common(p, nc, ctx, consts):
    t = {}
    t["cpool"] = ctx.enter_context(
        nc.sbuf_tensor("cpool", [128, max(1, len(consts))], F32))
    for i, cv in enumerate(consts):
        nc.const_aps.register(F32, cv, t["cpool"][:, i:i + 1])
    t["id_bf"] = ctx.enter_context(nc.sbuf_tensor("id_bf", [128, 128], BF16))
    t["id_f32"] = ctx.enter_context(nc.sbuf_tensor("id_f32", [128, 128], F32))
    t["ones_bf"] = ctx.enter_context(nc.sbuf_tensor("ones_bf", [128, 1], BF16))
    t["ones_f32r"] = ctx.enter_context(nc.sbuf_tensor("ones_f32r", [1, 128], F32))

    evs = []
    last = None
    for i, cv in enumerate(consts):
        last = p.op("pool", lambda g, ap=t["cpool"][:, i:i + 1], v=cv:
                    g.memset(ap, v), tag="pool")
    if last:
        evs.append(last)
    e1 = p.op("pool", lambda g: g.memset(t["id_bf"][:], 0.0), tag="pool")
    e2 = p.op("pool", lambda g: g.affine_select(
        out=t["id_bf"][:], in_=t["id_bf"][:], compare_op=ALU.not_equal,
        fill=1.0, base=0, pattern=[[-1, 128]], channel_multiplier=1),
        waits=[e1], tag="pool")
    e3 = p.op("pool", lambda g: g.memset(t["id_f32"][:], 0.0), tag="pool")
    e4 = p.op("pool", lambda g: g.affine_select(
        out=t["id_f32"][:], in_=t["id_f32"][:], compare_op=ALU.not_equal,
        fill=1.0, base=0, pattern=[[-1, 128]], channel_multiplier=1),
        waits=[e3], tag="pool")
    e5 = p.op("pool", lambda g: g.memset(t["ones_bf"][:], 1.0), tag="pool")
    e6 = p.op("pool", lambda g: g.memset(t["ones_f32r"][:], 1.0), tag="pool")
    evs += [e2, e4, e5, e6]
    t["ev_setup"] = evs
    return t



def build_fused(cfg, dbg_spills=False):
    cfg = _derive(cfg)
    S, D, ND, NH, NKV, G = (cfg[k] for k in ("S", "D", "ND", "NH", "NKV", "G"))
    NG, NC, NJ, NSLAB = (cfg[k] for k in ("NG", "NC", "NJ", "NSLAB"))
    H, SQ2, NT2, NDO = (cfg[k] for k in ("H", "SQ2", "NT2", "NDO"))
    E = E_EXPERTS
    DKV = NKV * 128
    invD = 1.0 / D

    nc = bass.Bass("TRN2", target_bir_lowering=False)
    x_in = nc.dram_tensor("x_in", [S, D], F32, kind="ExternalInput").ap()
    wq = nc.dram_tensor("wq", [D, NH * 128], F32, kind="ExternalInput").ap()
    wk = nc.dram_tensor("wk", [D, DKV], F32, kind="ExternalInput").ap()
    wv = nc.dram_tensor("wv", [D, DKV], F32, kind="ExternalInput").ap()
    x_my = nc.dram_tensor("x_my", [SQ2, D], F32, kind="ExternalInput").ap()
    wo = nc.dram_tensor("wo", [D, D], F32, kind="ExternalInput").ap()
    rwT = nc.dram_tensor("rwT", [D, E], F32, kind="ExternalInput").ap()
    rb = nc.dram_tensor("rb", [1, E], F32, kind="ExternalInput").ap()
    eb = nc.dram_tensor("eb", [E, D], F32, kind="ExternalInput").ap()
    y = nc.dram_tensor("y", [SQ2, D], F32, kind="ExternalOutput").ap()
    x1_s = nc.dram_tensor("x1_s", [SQ2, D], F32, kind="Internal").ap()
    send_s = nc.dram_tensor("send_s", [NH * 128, S], BF16, kind="Internal").ap()
    recv_t = nc.dram_tensor("recv_s", [H * 128, S], BF16, kind="Internal")
    recv_s = recv_t.ap()
    _spk = "ExternalOutput" if dbg_spills else "Internal"
    xfm_s = nc.dram_tensor("xfm_s", [D, S], BF16, kind=_spk).ap()
    kq_s = nc.dram_tensor("kq_s", [DKV, S], BF16, kind=_spk).ap()
    v_s = nc.dram_tensor("v_s", [S, DKV], BF16, kind=_spk).ap()
    q_s = nc.dram_tensor("q_s", [NH * 128, S], BF16, kind=_spk).ap()

    consts = [float(-np.log(THETA) / 64.0), float(-np.log(2 * np.pi)),
              float(-2.0 * np.pi), float(EPS), float(CSAFE * CSAFE / 128.0),
              float(1.0 / np.sqrt(128.0)), float(np.pi)]

    from contextlib import ExitStack
    with ExitStack() as octx, nc.Block() as block:
        p = Prog(nc)
        for gd in p.sem_ctx:
            octx.push(gd)
        t = _setup_common(p, nc, octx, consts)
        id_bf, id_f32 = t["id_bf"][:], t["id_f32"][:]
        ones_bf, ones_f32r = t["ones_bf"][:], t["ones_f32r"][:]
        ev_setup = t["ev_setup"]

        r_cols = octx.enter_context(nc.sbuf_tensor("r_cols", [128, cfg["NT"]], F32))
        mk_max = octx.enter_context(nc.sbuf_tensor("mk_max", [1, NKV], F32))
        mq_max = octx.enter_context(nc.sbuf_tensor("mq_max", [1, NH], F32))
        cbias = octx.enter_context(nc.sbuf_tensor("cbias", [128, NH], F32))

        with ExitStack() as p1ctx:
            cossin = p1ctx.enter_context(nc.sbuf_tensor("cossin", [128, S], F32))
            cossin2 = p1ctx.enter_context(nc.sbuf_tensor("cossin2", [128, S], F32))
            mt1 = p1ctx.enter_context(nc.sbuf_tensor("mt1", [128, 128], BF16))
            mt2 = p1ctx.enter_context(nc.sbuf_tensor("mt2", [128, 128], BF16))
            negmu_bf = p1ctx.enter_context(nc.sbuf_tensor("negmu_bf", [1, S], BF16))
            r_row = p1ctx.enter_context(nc.sbuf_tensor("r_row", [1, S], F32))
            s_wk = p1ctx.enter_context(nc.sbuf_tensor("s_wk", [1, DKV], BF16))
            s_wv = p1ctx.enter_context(nc.sbuf_tensor("s_wv", [1, DKV], BF16))
            s_wq = p1ctx.enter_context(nc.sbuf_tensor("s_wq", [1, NH * 128], BF16))

            em0 = p.op("pool", lambda g: g.memset(mt1[:], 0.0), tag="pool")
            em1 = p.op("pool", lambda g: g.affine_select(
                out=mt1[:], in_=mt1[:], compare_op=ALU.not_equal, fill=1.0,
                base=0, pattern=[[-1, 128]], channel_multiplier=1),
                waits=[em0], tag="pool")
            em2 = p.op("pool", lambda g: g.affine_select(
                out=mt1[:], in_=mt1[:], compare_op=ALU.not_equal, fill=-1.0,
                base=-64, pattern=[[-1, 128]], channel_multiplier=1),
                waits=[em1], tag="pool")
            em3 = p.op("pool", lambda g: g.memset(mt1[:, 64:128], 0.0),
                       waits=[em2], tag="pool")
            em4 = p.op("pool", lambda g: g.memset(mt2[:], 0.0), tag="pool")
            em5 = p.op("pool", lambda g: g.affine_select(
                out=mt2[:], in_=mt2[:], compare_op=ALU.not_equal, fill=1.0,
                base=64, pattern=[[-1, 128]], channel_multiplier=1),
                waits=[em4], tag="pool")
            em6 = p.op("pool", lambda g: g.affine_select(
                out=mt2[:], in_=mt2[:], compare_op=ALU.not_equal, fill=1.0,
                base=0, pattern=[[-1, 128]], channel_multiplier=1),
                waits=[em5], tag="pool")
            em7 = p.op("pool", lambda g: g.memset(mt2[:, 0:64], 0.0),
                       waits=[em6], tag="pool")
            ev_mt = [em3, em7]

            with ExitStack() as _st:
                rr = _st.enter_context(nc.sbuf_tensor("rr", [128, S], F32))
                work = _st.enter_context(nc.sbuf_tensor("work", [128, S], F32))
                flo = _st.enter_context(nc.sbuf_tensor("flo", [128, S], F32))
                ropei = _st.enter_context(nc.sbuf_tensor("ropei", [128, S], I32))
                jcol = _st.enter_context(nc.sbuf_tensor("jcol", [128, 1], F32))
                invf = _st.enter_context(nc.sbuf_tensor("invf", [128, 1], F32))
                ei1 = p.op("pool", lambda g: g.iota(
                    jcol[:], pattern=[[0, 1]], base=0, channel_multiplier=1,
                    allow_small_or_imprecise_dtypes=True), tag="pool")
                ei1b = p.op("pool", lambda g: g.iota(
                    jcol[64:128, :], pattern=[[0, 1]], base=0,
                    channel_multiplier=1,
                    allow_small_or_imprecise_dtypes=True),
                    waits=[ei1], tag="pool")
                ei2 = p.op("pool", lambda g: g.iota(
                    rr[:], pattern=[[1, S]], base=0, channel_multiplier=0,
                    allow_small_or_imprecise_dtypes=True), tag="pool")
                ea = p.op("act", lambda s: s.activation(
                    out=invf[:], in_=jcol[:], func=AF.Exp,
                    scale=consts[0], bias=consts[1]),
                    waits=[ei1, ei1b] + ev_setup, tag="act")
                ev1 = p.op("dve", lambda v: v.tensor_scalar_mul(
                    rr[:], rr[:], invf[:]), waits=[ea, ei2], tag="dve")
                ev_tab = None
                prev = [ev1]
                for dst, halfs in ((cossin, slice(0, 64)),
                                   (cossin2, slice(64, 128))):
                    a1 = p.op("dve", lambda v, dst=dst: v.tensor_copy(
                        work[:], rr[:]), waits=prev, tag="dve")
                    a2 = p.op("dve", lambda v, halfs=halfs: v.tensor_scalar_add(
                        work[halfs, :], work[halfs, :], 0.25),
                        waits=[a1], tag="dve")
                    a3 = p.op("dve", lambda v: v.tensor_copy(ropei[:], work[:]),
                              waits=[a2], tag="dve")
                    a4 = p.op("dve", lambda v: v.tensor_copy(flo[:], ropei[:]),
                              waits=[a3], tag="dve")
                    a5a = p.op("dve", lambda v: v.tensor_sub(
                        work[:], work[:], flo[:]), waits=[a4], tag="dve")
                    a5b = p.op("dve", lambda v: v.tensor_single_scalar(
                        flo[:], work[:], 0.0, ALU.is_lt), waits=[a5a], tag="dve")
                    a5 = p.op("dve", lambda v: v.tensor_add(
                        work[:], work[:], flo[:]), waits=[a5b], tag="dve")
                    ev_tab = p.op("act", lambda s, dst=dst: s.activation(
                        out=dst[:], in_=work[:], func=AF.Sin, scale=consts[2],
                        bias=consts[6]), waits=[a5], tag="act")
                    prev = [ev_tab, a5]
                p.flush(block)
            p.set_barrier()

            xfm_done = [None] * NG

            with ExitStack() as _st:
                wk_sb = _st.enter_context(nc.sbuf_tensor("wk_sb", [128, ND * 512], BF16))
                wv_sb = _st.enter_context(nc.sbuf_tensor("wv_sb", [128, ND * 512], BF16))
                xt = _st.enter_context(nc.sbuf_tensor("xt", [128, 2 * D], F32))
                xbf8 = _st.enter_context(nc.sbuf_tensor("xbf8", [128, 2 * 1024], BF16))
                xfmg = _st.enter_context(nc.sbuf_tensor("xfmg", [128, ND * 512], BF16))
                stat = _st.enter_context(nc.sbuf_tensor("stat", [128, 32], F32))
                trash_bf = _st.enter_context(nc.sbuf_tensor("trash_bf", [128, 1024], BF16))
                sq4 = _st.enter_context(nc.sbuf_tensor("sq4", [128, 16], F32))
                p1t = _st.enter_context(nc.sbuf_tensor("p1t", [128, 512], BF16))
                p2t = _st.enter_context(nc.sbuf_tensor("p2t", [128, 512], BF16))
                kbf = _st.enter_context(nc.sbuf_tensor("kbf", [128, 2 * 512], BF16))
                vbf = _st.enter_context(nc.sbuf_tensor("vbf", [128, 2 * 512], BF16))
                sqbf = _st.enter_context(nc.sbuf_tensor("sqbf", [128, 512], BF16))
                nmx = _st.enter_context(nc.sbuf_tensor("nmx", [1, 8], F32))
                rbc = _st.enter_context(nc.sbuf_tensor("rbc", [128, 512], F32))
                ps_tr = _st.enter_context(nc.psum_tensor("ps_tr", [128, 2 * 1024], BF16))
                ps_st = _st.enter_context(nc.psum_tensor("ps_st", [128, 512], F32))
                ps_rb = _st.enter_context(nc.psum_tensor("ps_rb", [128, 512], F32))
                ps_nrm = _st.enter_context(nc.psum_tensor("ps_nrm", [128, 512], F32))
                ps_kv = _st.enter_context(nc.psum_tensor("ps_kv", [128, 3 * 512], F32))

                wk_v = wk.rearrange("(d p) w -> p d w", p=128)
                wv_v = wv.rearrange("(d p) w -> p d w", p=128)
                ev_wk = p.op("pool", lambda g: g.dma_start(
                    out=wk_sb[:].rearrange("p (d w) -> p d w", w=512)[:, :, 0:DKV],
                    in_=wk_v), tag=p.q("qwk"))
                ev_wv = p.op("pool", lambda g: g.dma_start(
                    out=wv_sb[:].rearrange("p (d w) -> p d w", w=512)[:, :, 0:DKV],
                    in_=wv_v), tag=p.q("qwv"))
                ev_swk = None
                for d in range(ND):
                    ev_swk = p.op("pe", lambda te, d=d: te.matmul(
                        ps_st[0:1, 0:DKV], ones_bf[:],
                        wk_sb[:, d * 512:d * 512 + DKV],
                        start=(d == 0), stop=(d == ND - 1)),
                        waits=[ev_wk] + ev_setup,
                        tag="pe" if d == ND - 1 else None)
                ev_swk_c = p.op("dve", lambda v: v.tensor_copy(
                    s_wk[:], ps_st[0:1, 0:DKV]), waits=[ev_swk], tag="dve")
                ev_swv = None
                for d in range(ND):
                    ev_swv = p.op("pe", lambda te, d=d: te.matmul(
                        ps_nrm[0:1, 0:DKV], ones_bf[:],
                        wv_sb[:, d * 512:d * 512 + DKV],
                        start=(d == 0), stop=(d == ND - 1)),
                        waits=[ev_wv], tag="pe" if d == ND - 1 else None)
                ev_swv_c = p.op("dve", lambda v: v.tensor_copy(
                    s_wv[:], ps_nrm[0:1, 0:DKV]), waits=[ev_swv], tag="dve")
                ev_mki = p.op("dve", lambda v: v.memset(mk_max[:], 0.0), tag="dve")
                ev_mqi = p.op("dve", lambda v: v.memset(mq_max[:], 0.0), tag="dve")

                x_readers = {0: [], 1: []}
                xbf_readers = {0: [], 1: []}
                ps_tr_evac = {0: None, 1: None}
                ps_st_war = [ev_swk_c]
                ps_rb_war = []
                rbc_war = []
                ps_nrm_war = [ev_swv_c]
                kv_war = {0: [], 1: [], 2: []}
                kbf_war = {0: [], 1: []}
                vbf_war = {0: [], 1: []}
                nmx_war = [ev_mki, ev_mqi]
                pending_rope = [None]
                stat_fins = []
                trash_war = []
                sqbf_war = []
                p1_war = []
                p2_war = []
                rope_ps_war = [ev_swv_c]
                xfmg_war = []

                for g in range(NG):
                    ev_xfm_tiles = []
                    ev_r_tiles = []
                    ev_negmu_tiles = []
                    for tt in range(4):
                        ti = g * 4 + tt
                        par = ti % 2
                        p4 = ti % 4
                        xt_v = xt[:, par * D:(par + 1) * D]
                        ev_l = p.op("sp", lambda s, ti=ti, xt_v=xt_v: s.dma_start(
                            out=xt_v, in_=x_in[ti * 128:(ti + 1) * 128, :]),
                            waits=x_readers[par], tag=p.q(f"qx{par}"))
                        c_sum = stat[:, p4 * 8 + 0: p4 * 8 + 1]
                        c_sq = stat[:, p4 * 8 + 1: p4 * 8 + 2]
                        c_negmu = stat[:, p4 * 8 + 2: p4 * 8 + 3]
                        c_mu2 = stat[:, p4 * 8 + 3: p4 * 8 + 4]
                        c_var = stat[:, p4 * 8 + 4: p4 * 8 + 5]
                        c_sd = stat[:, p4 * 8 + 5: p4 * 8 + 6]
                        e1 = p.op("dve", lambda v, a=c_sum, b=xt_v: v.reduce_sum(
                            a, b, axis=AXX), waits=[ev_l], tag="dve")
                        q4w = min(1024, D)
                        nq4 = D // q4w
                        e2c = None
                        for q4 in range(nq4):
                            e2c = p.op("act", lambda s, q4=q4, b=xt_v, p4=p4,
                                       q4w=q4w: s.activation(
                                           out=trash_bf[:, 0:q4w],
                                           in_=b[:, q4 * q4w:(q4 + 1) * q4w],
                                           func=AF.Square,
                                           accum_out=sq4[:, p4 * 4 + q4:
                                                         p4 * 4 + q4 + 1]),
                                       waits=[ev_l] + trash_war, tag="act")
                            trash_war = [e2c]
                        e2 = p.op("dve", lambda v, a=c_sq, p4=p4: v.reduce_sum(
                            a, sq4[:, p4 * 4:p4 * 4 + nq4], axis=AXX),
                            waits=[e2c], tag="dve")
                        e3 = p.op("dve", lambda v, a=c_negmu, b=c_sum:
                                  v.tensor_scalar_mul(a, b, -invD),
                                  waits=[e1], tag="dve")
                        e4 = p.op("dve", lambda v, a=c_mu2, b=c_negmu:
                                  v.tensor_mul(a, b, b), waits=[e3], tag="dve")
                        e5 = p.op("dve", lambda v, a=c_var, b=c_sq, c=c_mu2:
                                  v.scalar_tensor_tensor(
                                      out=a, in0=b, scalar=invD, in1=c,
                                      op0=ALU.mult, op1=ALU.subtract),
                                  waits=[e2, e4], tag="dve")
                        e6 = p.op("act", lambda s, a=c_sd, b=c_var: s.activation(
                            out=a, in_=b, func=AF.Sqrt, bias=consts[3]),
                            waits=[e5], tag="act")
                        rc = r_cols[:, ti:ti + 1]
                        e7 = p.op("dve", lambda v, a=rc, b=c_sd: v.reciprocal(a, b),
                                  waits=[e6], tag="dve")

                        def stat_fin(ti=ti, e3=e3, e7=e7, c_negmu=c_negmu,
                                     rc=rc):
                            nonlocal ps_st_war
                            e8 = p.op("pe", lambda te, b=c_negmu: te.transpose(
                                ps_st[0:1, 0:128], b, id_f32),
                                waits=[e3] + ps_st_war, tag="pe")
                            e9 = p.op("dve", lambda v, ti=ti: v.tensor_copy(
                                negmu_bf[0:1, ti * 128:(ti + 1) * 128],
                                ps_st[0:1, 0:128]), waits=[e8], tag="dve")
                            e10 = p.op("pe", lambda te, b=rc: te.transpose(
                                ps_st[0:1, 128:256], b, id_f32),
                                waits=[e7, e9], tag="pe")
                            e11 = p.op("dve", lambda v, ti=ti: v.tensor_copy(
                                r_row[0:1, ti * 128:(ti + 1) * 128],
                                ps_st[0:1, 128:256]), waits=[e10], tag="dve")
                            ps_st_war = [e9, e11]
                            ev_r_tiles.append(e11)
                            ev_negmu_tiles.append(e9)
                        stat_fins.append(stat_fin)
                        x_cast_readers = []
                        NB8 = min(8, ND)
                        for bk in range(ND // NB8):
                            xbv = xbf8[:, (bk % 2) * 1024:
                                       (bk % 2) * 1024 + NB8 * 128]
                            ecast = p.op("dve", lambda v, a=xbv, b=xt_v, bk=bk,
                                         NB8=NB8: v.tensor_copy(
                                             a, b[:, bk * NB8 * 128:
                                                  (bk + 1) * NB8 * 128]),
                                         waits=[ev_l] + xbf_readers[bk % 2],
                                         tag="dve")
                            x_cast_readers.append(ecast)
                            bank = ps_tr[:, (bk % 2) * 1024:
                                         (bk % 2) * 1024 + NB8 * 128]
                            last_tr = None
                            for dd in range(NB8):
                                w = [ecast]
                                if dd == 0 and ps_tr_evac[bk % 2] is not None:
                                    w.append(ps_tr_evac[bk % 2])
                                last_tr = p.op("pe", lambda te, dd=dd,
                                               bank=bank, xbv=xbv:
                                               te.transpose(
                                                   bank[:, dd * 128:(dd + 1) * 128],
                                                   xbv[:, dd * 128:(dd + 1) * 128],
                                                   id_bf),
                                               waits=w,
                                               tag="pe" if dd == NB8 - 1 else None)
                            xbf_readers[bk % 2] = [last_tr]
                            dst = xfmg[:].rearrange("p (d w) -> p d w", w=512)[
                                :, bk * NB8:(bk + 1) * NB8,
                                tt * 128:(tt + 1) * 128]
                            eev = p.op("act", lambda s, bank=bank, dst=dst: s.copy(
                                out=dst,
                                in_=bank.rearrange("p (d w) -> p d w", w=128)),
                                waits=[last_tr] + xfmg_war, tag="act")
                            ps_tr_evac[bk % 2] = eev
                            ev_xfm_tiles.append(eev)
                        x_readers[par] = [e1, e2] + x_cast_readers

                    ev_xsp = p.op("sp", lambda s, g=g: s.dma_start(
                        out=xfm_s.rearrange("(d p) s -> p d s", p=128)[
                            :, :, g * 512:(g + 1) * 512],
                        in_=xfmg[:].rearrange("p (d w) -> p d w", w=512)),
                        waits=ev_xfm_tiles, tag=p.q("qxs"))
                    xfm_done[g] = ev_xsp

                    for sf in stat_fins:
                        sf()
                    stat_fins.clear()
                    erb = p.op("pe", lambda te, g=g: te.matmul(
                        ps_rb[:], ones_f32r[:], r_row[0:1, g * 512:(g + 1) * 512],
                        start=True, stop=True),
                        waits=ev_r_tiles + ps_rb_war, tag="pe")
                    erbc = p.op("dve", lambda v: v.tensor_copy(rbc[:], ps_rb[:]),
                                waits=[erb] + rbc_war, tag="dve")
                    ps_rb_war = [erbc]
                    rbc_war = []

                    k_group_folds = []
                    for kd in range(NKV):
                        bank = kd % 2
                        kps = ps_kv[:, bank * 512:bank * 512 + 512]
                        for d in range(ND):
                            w = ([ev_wk] + ev_xfm_tiles + kv_war[bank]
                                 if d == 0 else [])
                            p.op("pe", lambda te, d=d, kd=kd, kps=kps: te.matmul(
                                kps,
                                wk_sb[:, d * 512 + kd * 128:d * 512 + kd * 128 + 128],
                                xfmg[:, d * 512:(d + 1) * 512],
                                start=(d == 0), stop=False), waits=w)
                            if d == min(8, ND - 1) and \
                                    pending_rope[0] is not None:
                                pending_rope[0]()
                                pending_rope[0] = None
                        efold = p.op("pe", lambda te, kd=kd, g=g, kps=kps:
                                     te.matmul(
                                         kps, s_wk[0:1, kd * 128:kd * 128 + 128],
                                         negmu_bf[0:1, g * 512:(g + 1) * 512],
                                         start=False, stop=True),
                                     waits=[ev_swk_c] + ev_negmu_tiles, tag="pe")
                        k_group_folds.append(efold)

                        def rope_tail(kd=kd, g=g, kps=kps, bank=bank,
                                      efold=efold, erbc=erbc):
                            nonlocal p1_war, p2_war, sqbf_war, ps_nrm_war, \
                                rope_ps_war, nmx_war
                            csg = cossin[:, g * 512:(g + 1) * 512]
                            scg = cossin2[:, g * 512:(g + 1) * 512]
                            kb = kbf[:, (kd % 2) * 512:(kd % 2) * 512 + 512]
                            eP1 = p.op("dve", lambda v, csg=csg, kps=kps:
                                       v.tensor_tensor(
                                           out=p1t[:], in0=kps, in1=csg,
                                           op=ALU.mult),
                                       waits=[efold, ev_tab] + p1_war,
                                       tag="dve")
                            eP2 = p.op("dve", lambda v, scg=scg, kps=kps:
                                       v.tensor_tensor(
                                           out=p2t[:], in0=kps, in1=scg,
                                           op=ALU.mult),
                                       waits=[efold, ev_tab] + p2_war,
                                       tag="dve")
                            kv_war[bank] = [eP1, eP2]
                            eM1 = p.op("pe", lambda te: te.matmul(
                                ps_nrm[:], mt1[:], p1t[:], start=True,
                                stop=False),
                                waits=[eP1] + ev_mt + rope_ps_war)
                            eM2 = p.op("pe", lambda te: te.matmul(
                                ps_nrm[:], mt2[:], p2t[:], start=False,
                                stop=True), waits=[eP2], tag="pe")
                            p1_war = [eM2]
                            p2_war = [eM2]
                            eEv = p.op("dve", lambda v, kb=kb: v.tensor_tensor(
                                out=kb, in0=ps_nrm[:], in1=rbc[:],
                                op=ALU.mult),
                                waits=[eM2, erbc] + kbf_war[kd % 2],
                                tag="dve")
                            rbc_war.append(eEv)
                            en1 = p.op("act", lambda s, kb=kb: s.activation(
                                out=sqbf[:], in_=kb, func=AF.Square),
                                waits=[eEv] + sqbf_war, tag="act")
                            en2 = p.op("pe", lambda te: te.matmul(
                                ps_nrm[0:1, :], ones_bf[:], sqbf[:],
                                start=True, stop=True),
                                waits=[en1, eEv] + ps_nrm_war, tag="pe")
                            sqbf_war = [en2]
                            en3 = p.op("dve", lambda v: v.reduce_max(
                                nmx[0:1, 0:1], ps_nrm[0:1, :], axis=AXX),
                                waits=[en2] + nmx_war, tag="dve")
                            en4 = p.op("dve", lambda v, kd=kd: v.tensor_max(
                                mk_max[0:1, kd:kd + 1], mk_max[0:1, kd:kd + 1],
                                nmx[0:1, 0:1]), waits=[en3], tag="dve")
                            ps_nrm_war = []
                            rope_ps_war = [en3]
                            nmx_war = [en4]
                            esp = p.op("sp", lambda s, kd=kd, g=g, kb=kb:
                                       s.dma_start(
                                           out=kq_s[kd * 128:(kd + 1) * 128,
                                                    g * 512:(g + 1) * 512],
                                           in_=kb), waits=[eEv],
                                       tag=p.q(f"qks{kd % 2}"))
                            kbf_war[kd % 2] = [en1, esp]
                        assert pending_rope[0] is None
                        pending_rope[0] = rope_tail

                    v_group_folds = []
                    for tt in range(4):
                        ti = g * 4 + tt
                        vps = ps_kv[:, 2 * 512:2 * 512 + DKV]
                        for d in range(ND):
                            w = ([ev_wv] + ev_xfm_tiles + kv_war[2]
                                 if d == 0 else [])
                            p.op("pe", lambda te, d=d, tt=tt, vps=vps: te.matmul(
                                vps,
                                xfmg[:, d * 512 + tt * 128:d * 512 + tt * 128 + 128],
                                wv_sb[:, d * 512:d * 512 + DKV],
                                start=(d == 0), stop=False), waits=w)
                            if d == min(8, ND - 1) and \
                                    pending_rope[0] is not None:
                                pending_rope[0]()
                                pending_rope[0] = None
                        efold = p.op("pe", lambda te, ti=ti, vps=vps: te.matmul(
                            vps, negmu_bf[0:1, ti * 128:(ti + 1) * 128],
                            s_wv[0:1, :], start=False, stop=True),
                            waits=[ev_swv_c, ev_negmu_tiles[tt]], tag="pe")
                        v_group_folds.append(efold)
                        vb = vbf[:, (tt % 2) * 512:(tt % 2) * 512 + DKV]
                        eve = p.op("dve", lambda v, ti=ti, vps=vps, vb=vb:
                                   v.tensor_scalar(
                                       out=vb, in0=vps,
                                       scalar1=r_cols[:, ti:ti + 1],
                                       scalar2=None, op0=ALU.mult),
                                   waits=[efold] + vbf_war[tt % 2], tag="dve")
                        kv_war[2] = [eve]
                        evs_ = p.op("sp", lambda s, ti=ti, vb=vb: s.dma_start(
                            out=v_s[ti * 128:(ti + 1) * 128, :], in_=vb),
                            waits=[eve], tag=p.q(f"qvs{tt % 2}"))
                        vbf_war[tt % 2] = [evs_]

                    xfmg_war = [k_group_folds[-1], v_group_folds[-1], ev_xsp]
                p.flush(block)
            p.set_barrier()

            with ExitStack() as _st:
                wq_sb = _st.enter_context(nc.sbuf_tensor("wq_sb", [128, ND * 1024], BF16))
                xfml = _st.enter_context(nc.sbuf_tensor("xfml", [128, 2 * ND * 512], BF16))
                p1q = _st.enter_context(nc.sbuf_tensor("p1q", [128, 512], BF16))
                p2q = _st.enter_context(nc.sbuf_tensor("p2q", [128, 512], BF16))
                qbf = _st.enter_context(nc.sbuf_tensor("qbf", [128, 2 * 512], BF16))
                sqbf2 = _st.enter_context(nc.sbuf_tensor("sqbf2", [128, 512], BF16))
                nmx2 = _st.enter_context(nc.sbuf_tensor("nmx2", [1, 8], F32))
                rbc2 = _st.enter_context(nc.sbuf_tensor("rbc2", [128, 512], F32))
                crow = _st.enter_context(nc.sbuf_tensor("crow", [1, max(NH, 1)], F32))
                ps_q = _st.enter_context(nc.psum_tensor("ps_q", [128, 2 * 512], F32))
                ps_rb2 = _st.enter_context(nc.psum_tensor("ps_rb2", [128, 512], F32))
                ps_nq = _st.enter_context(nc.psum_tensor("ps_nq", [128, 512], F32))
                ps_sq = _st.enter_context(nc.psum_tensor("ps_sq", [128, 512], F32))

                wq_v = wq.rearrange("(d p) w -> p d w", p=128)
                nhps = min(8, NH)
                ev_wq = None
                ev_swq_c = None
                wq_readers = []
                xf_war = {0: [], 1: []}
                q_war = {0: [], 1: []}
                qbf_war = {0: [], 1: []}
                rb2_war = []
                rbc2_war = []
                nq_war = []
                sq_war = []
                nmx2_war = []
                p1q_war = []
                p2q_war = []
                rope_q_war = []
                pending_q = [None]
                ev_mqi2 = p.op("dve", lambda v: v.memset(nmx2[:], 0.0), tag="dve")
                nmx2_war = [ev_mqi2]

                for slab in range(NSLAB):
                    h0 = slab * nhps
                    nh_s = min(nhps, NH - h0)
                    wslab = wq_v[:, :, h0 * 128:(h0 + nh_s) * 128]
                    ev_wq = p.op("pool", lambda g, wslab=wslab, nh_s=nh_s:
                                 g.dma_start(
                                     out=wq_sb[:].rearrange(
                                         "p (d w) -> p d w", w=1024)[
                                         :, :, 0:nh_s * 128],
                                     in_=wslab),
                                 waits=wq_readers, tag=p.q("qwq"))
                    ev_swq_parts = []
                    for hf in range((nh_s * 128 + 511) // 512):
                        w0 = min(512, nh_s * 128 - hf * 512)
                        lastc = None
                        for d in range(ND):
                            w = [ev_wq] + sq_war if d == 0 else []
                            lastc = p.op("pe", lambda te, d=d, hf=hf, w0=w0:
                                         te.matmul(
                                             ps_sq[0:1, 0:w0], ones_bf[:],
                                             wq_sb[:, d * 1024 + hf * 512:
                                                   d * 1024 + hf * 512 + w0],
                                             start=(d == 0),
                                             stop=(d == ND - 1)),
                                         waits=w,
                                         tag="pe" if d == ND - 1 else None)
                        ecp = p.op("dve", lambda v, h0=h0, hf=hf, w0=w0:
                                   v.tensor_copy(
                                       s_wq[0:1, h0 * 128 + hf * 512:
                                            h0 * 128 + hf * 512 + w0],
                                       ps_sq[0:1, 0:w0]),
                                   waits=[lastc], tag="dve")
                        sq_war = [ecp]
                        ev_swq_parts.append(ecp)
                    ev_swq_c = ev_swq_parts[-1]
                    wq_readers = []

                    for g in range(NG):
                        par = g % 2
                        xfv = xfml[:, par * ND * 512:(par + 1) * ND * 512]
                        ev_xl = p.op("sp", lambda s, g=g, xfv=xfv: s.dma_start(
                            out=xfv.rearrange("p (d w) -> p d w", w=512),
                            in_=xfm_s.rearrange("(d p) s -> p d s", p=128)[
                                :, :, g * 512:(g + 1) * 512]),
                            waits=[xfm_done[g]] + xf_war[par], tag=p.q(f"qxl{par}"))
                        erb = p.op("pe", lambda te, g=g: te.matmul(
                            ps_rb2[:], ones_f32r[:],
                            r_row[0:1, g * 512:(g + 1) * 512],
                            start=True, stop=True), waits=rb2_war, tag="pe")
                        erbc = p.op("dve", lambda v: v.tensor_copy(
                            rbc2[:], ps_rb2[:]), waits=[erb] + rbc2_war,
                            tag="dve")
                        rb2_war = [erbc]
                        rbc2_war = []
                        readers = []
                        for hh in range(nh_s):
                            h = h0 + hh
                            bank = hh % 2
                            qps = ps_q[:, bank * 512:bank * 512 + 512]
                            for d in range(ND):
                                w = ([ev_xl, ev_wq] + q_war[bank]
                                     if d == 0 else [])
                                p.op("pe", lambda te, d=d, hh=hh, qps=qps,
                                     xfv=xfv: te.matmul(
                                         qps,
                                         wq_sb[:, d * 1024 + hh * 128:
                                               d * 1024 + hh * 128 + 128],
                                         xfv[:, d * 512:(d + 1) * 512],
                                         start=(d == 0), stop=False), waits=w)
                                if d == min(8, ND - 1) and \
                                        pending_q[0] is not None:
                                    pending_q[0]()
                                    pending_q[0] = None
                            efold = p.op("pe", lambda te, h=h, g=g, qps=qps:
                                         te.matmul(
                                             qps,
                                             s_wq[0:1, h * 128:h * 128 + 128],
                                             negmu_bf[0:1,
                                                      g * 512:(g + 1) * 512],
                                             start=False, stop=True),
                                         waits=[ev_swq_c], tag="pe")
                            wq_readers.append(efold)

                            def q_tail(h=h, g=g, hh=hh, bank=bank, qps=qps,
                                       efold=efold, erbc=erbc,
                                       readers=readers):
                                nonlocal p1q_war, p2q_war, rope_q_war, \
                                    nq_war, nmx2_war
                                csg = cossin[:, g * 512:(g + 1) * 512]
                                scg = cossin2[:, g * 512:(g + 1) * 512]
                                qb = qbf[:, (hh % 2) * 512:(hh % 2) * 512 + 512]
                                eP1 = p.op("dve", lambda v, csg=csg, qps=qps:
                                           v.tensor_tensor(
                                               out=p1q[:], in0=qps, in1=csg,
                                               op=ALU.mult),
                                           waits=[efold, ev_tab] + p1q_war,
                                           tag="dve")
                                eP2 = p.op("dve", lambda v, scg=scg, qps=qps:
                                           v.tensor_tensor(
                                               out=p2q[:], in0=qps, in1=scg,
                                               op=ALU.mult),
                                           waits=[efold, ev_tab] + p2q_war,
                                           tag="dve")
                                q_war[bank] = [eP1, eP2]
                                eM1 = p.op("pe", lambda te: te.matmul(
                                    ps_nq[:], mt1[:], p1q[:], start=True,
                                    stop=False),
                                    waits=[eP1] + ev_mt + rope_q_war)
                                eM2 = p.op("pe", lambda te: te.matmul(
                                    ps_nq[:], mt2[:], p2q[:], start=False,
                                    stop=True), waits=[eP2], tag="pe")
                                p1q_war = [eM2]
                                p2q_war = [eM2]
                                eEv = p.op("dve", lambda v, qb=qb:
                                           v.tensor_tensor(
                                               out=qb, in0=ps_nq[:],
                                               in1=rbc2[:], op=ALU.mult),
                                           waits=[eM2, erbc] + qbf_war[hh % 2],
                                           tag="dve")
                                rbc2_war.append(eEv)
                                en1 = p.op("act", lambda s, qb=qb: s.activation(
                                    out=sqbf2[:], in_=qb, func=AF.Square),
                                    waits=[eEv] + nq_war, tag="act")
                                en2 = p.op("pe", lambda te: te.matmul(
                                    ps_nq[0:1, :], ones_bf[:], sqbf2[:],
                                    start=True, stop=True),
                                    waits=[en1, eEv], tag="pe")
                                en3 = p.op("dve", lambda v: v.reduce_max(
                                    nmx2[0:1, 0:1], ps_nq[0:1, :], axis=AXX),
                                    waits=[en2] + nmx2_war, tag="dve")
                                en4 = p.op("dve", lambda v, h=h: v.tensor_max(
                                    mq_max[0:1, h:h + 1],
                                    mq_max[0:1, h:h + 1],
                                    nmx2[0:1, 0:1]), waits=[en3, ev_mqi],
                                    tag="dve")
                                nq_war = [en2]
                                rope_q_war = [en3]
                                nmx2_war = [en4]
                                esp = p.op("sp", lambda s, h=h, g=g, qb=qb:
                                           s.dma_start(
                                               out=q_s[h * 128:(h + 1) * 128,
                                                       g * 512:(g + 1) * 512],
                                               in_=qb), waits=[eEv],
                                           tag=p.q(f"qqs{hh % 2}"))
                                qbf_war[hh % 2] = [en1, esp]
                                readers.append(esp)
                            assert pending_q[0] is None
                            pending_q[0] = q_tail
                        xf_war[par] = readers

                if pending_q[0] is not None:
                    pending_q[0]()
                    pending_q[0] = None
                ee1 = p.op("dve", lambda v: v.tensor_tensor(
                    out=crow[:].rearrange("o (k g) -> o k g", g=G),
                    in0=mq_max[:].rearrange("o (k g) -> o k g", g=G),
                    in1=mk_max[:].rearrange("o (k g) -> o k g", g=1
                                            ).to_broadcast([1, NKV, G]),
                    op=ALU.mult), waits=nmx2_war + nmx_war, tag="dve")
                ee2 = p.op("act", lambda s: s.activation(
                    out=crow[:], in_=crow[:], func=AF.Sqrt, scale=consts[4]),
                    waits=[ee1], tag="act")
                ee3 = p.op("dve", lambda v: v.tensor_scalar_mul(
                    crow[:], crow[:], -1.0), waits=[ee2], tag="dve")
                ee4 = p.op("pe", lambda te: te.matmul(
                    ps_q[:, 0:NH], ones_f32r[:], crow[:], start=True,
                    stop=True), waits=[ee3] + q_war[0], tag="pe")
                ev_cb = p.op("dve", lambda v: v.tensor_copy(
                    cbias[:], ps_q[:, 0:NH]), waits=[ee4], tag="dve")
                p.flush(block)
        p.set_barrier()

        with ExitStack() as _st:
            arena = _st.enter_context(nc.sbuf_tensor("arena", [128, NH * S], BF16))
            ksb = _st.enter_context(nc.sbuf_tensor("ksb", [128, 2 * S], BF16))
            vsb = _st.enter_context(nc.sbuf_tensor("vsb", [128, 2 * S], BF16))
            qt_sb = _st.enter_context(nc.sbuf_tensor("qt_sb", [128, 2 * 512], BF16))
            pt_sb = _st.enter_context(nc.sbuf_tensor("pt_sb", [128, 8 * 512], BF16))
            masks = _st.enter_context(nc.sbuf_tensor("masks", [128, 4 * 512], F32))
            rden = _st.enter_context(nc.sbuf_tensor("rden", [1, 512], F32))
            rden_b = _st.enter_context(nc.sbuf_tensor("rden_b", [128, 512], F32))
            ps_sc = _st.enter_context(nc.psum_tensor("ps_sc", [128, 3 * 512], F32))
            ps_av = _st.enter_context(nc.psum_tensor("ps_av", [128, 2 * 512], F32))
            ps_dn = _st.enter_context(nc.psum_tensor("ps_dn", [1, 2 * 512], F32))
            ps_rb3 = _st.enter_context(nc.psum_tensor("ps_rb3", [128, 512], F32))

            ev_msk = []
            for c in range(4):
                m = masks[:, c * 512:(c + 1) * 512]
                em1 = p.op("pool", lambda g, m=m: g.memset(m, 0.0), tag="pool")
                em2 = p.op("pool", lambda g, m=m, c=c: g.affine_select(
                    out=m, in_=m, compare_op=ALU.is_ge, fill=MASKVAL,
                    base=-(c * 128), pattern=[[1, 512]], channel_multiplier=-1),
                    waits=[em1], tag="pool")
                ev_msk.append(em2)

            k_war = {0: [], 1: []}
            v_war = {0: [], 1: []}
            q2_war = {0: [], 1: []}
            pt_war = [None] * 8
            sc_war = [[], [], []]
            av_war = [[], []]
            dnp_war = {0: [], 1: []}
            rb3_war = []
            rden_war = []
            rdenb_war = []
            arena_evs = []
            pending_fin = [None]

            v_v = v_s.rearrange("(c p) w -> p c w", p=128)
            groups = [[2 * b, 2 * b + 1] for b in range(cfg["B"])]
            NH2 = NH // 2
            cc_evs = []

            for hk in range(NKV):
                if hk == NKV // 2 and NKV > 1:
                    if pending_fin[0] is not None:
                        pending_fin[0]()
                        pending_fin[0] = None
                    ev_send1 = p.op("sp", lambda s: s.dma_start(
                        out=send_s[0:NH2 * 128, :].rearrange(
                            "(h p) s -> p h s", p=128),
                        in_=arena[:, 0:NH2 * S].rearrange(
                            "p (h s) -> p h s", s=S)),
                        waits=list(arena_evs), tag=p.q("qsend"))
                    cc_evs.append(p.op("pool", lambda g: g.collective_compute(
                        "AllGather", ALU.bypass, replica_groups=groups,
                        ins=[send_s[0:NH2 * 128, :].opt()],
                        outs=[recv_s[0:2 * NH2 * 128, :].opt()]),
                        waits=[ev_send1], tag=p.q1("qcc")))
                kpar = hk % 2
                ksl = ksb[:, kpar * S:(kpar + 1) * S]
                vsl = vsb[:, kpar * S:(kpar + 1) * S]
                ev_k = p.op("sp", lambda s, hk=hk, ksl=ksl: s.dma_start(
                    out=ksl, in_=kq_s[hk * 128:(hk + 1) * 128, :]),
                    waits=k_war[kpar], tag=p.q(f"qk{kpar}"))
                ev_v = p.op("sp", lambda s, hk=hk, vsl=vsl: s.dma_start(
                    out=vsl.rearrange("p (c w) -> p c w", w=128),
                    in_=v_v[:, :, hk * 128:(hk + 1) * 128]),
                    waits=v_war[kpar], tag=p.q(f"qv{kpar}"))
                k_readers = []
                v_readers = []
                for gq in range(G):
                    h = hk * G + gq
                    for J in range(NJ):
                        qpar = (hk * G * NJ + gq * NJ + J) % 2
                        qv = qt_sb[:, qpar * 512:(qpar + 1) * 512]
                        ev_q = p.op("sp", lambda s, h=h, J=J, qv=qv: s.dma_start(
                            out=qv, in_=q_s[h * 128:(h + 1) * 128,
                                            J * 512:(J + 1) * 512]),
                            waits=q2_war[qpar], tag=p.q(f"qq{qpar}"))
                        nch = 4 * (J + 1)
                        avb = (h * NJ + J) % 2
                        avp = ps_av[:, avb * 512:avb * 512 + 512]
                        dnp = ps_dn[0:1, avb * 512:avb * 512 + 512]
                        ev_avmm = None
                        ev_dnmm = None
                        last_exp = None
                        exp_evs = [None] * nch
                        SKEW = 2
                        for cc_i in range(nch + SKEW):
                            if cc_i < nch:
                                c = cc_i
                                scb = c % 3
                                scp = ps_sc[:, scb * 512:scb * 512 + 512]
                                w = [ev_k, ev_q] + sc_war[scb]
                                emm = p.op("pe", lambda te, c=c, scp=scp,
                                           ksl=ksl, qv=qv: te.matmul(
                                               scp,
                                               ksl[:, c * 128:(c + 1) * 128],
                                               qv, start=True, stop=True),
                                           waits=w, tag="pe")
                                wexp = [emm]
                                if c >= nch - 4:
                                    mcc = c - (nch - 4)
                                    emsk = p.op("dve", lambda v, scp=scp,
                                                mcc=mcc: v.tensor_tensor(
                                                    out=scp, in0=scp,
                                                    in1=masks[:, mcc * 512:(mcc + 1) * 512],
                                                    op=ALU.add),
                                                waits=[emm, ev_msk[mcc]],
                                                tag="dve")
                                    wexp = [emsk]
                                ptv = pt_sb[:, (c % 8) * 512:(c % 8) * 512 + 512]
                                wpt = [pt_war[c % 8]] if pt_war[c % 8] else []
                                eexp = p.op("act", lambda s, scp=scp, ptv=ptv,
                                            h=h: s.activation(
                                                out=ptv, in_=scp, func=AF.Exp,
                                                scale=consts[5],
                                                bias=cbias[:, h:h + 1]),
                                            waits=wexp + wpt + [ev_cb],
                                            tag="act")
                                exp_evs[c] = eexp
                                last_exp = eexp
                                sc_war[scb] = [eexp]
                            if cc_i >= SKEW:
                                c = cc_i - SKEW
                                ptv = pt_sb[:, (c % 8) * 512:(c % 8) * 512 + 512]
                                w_av = [exp_evs[c], ev_v]
                                if c == 0:
                                    w_av += av_war[avb]
                                ev_avmm = p.op("pe", lambda te, c=c, avp=avp,
                                               vsl=vsl, ptv=ptv, nch=nch:
                                               te.matmul(
                                                   avp,
                                                   vsl[:, c * 128:(c + 1) * 128],
                                                   ptv, start=(c == 0),
                                                   stop=(c == nch - 1)),
                                               waits=w_av,
                                               tag="pe" if c == nch - 1
                                               else None)
                                w_dn = [exp_evs[c]]
                                if c == 0:
                                    w_dn += dnp_war[avb]
                                ev_dnmm = p.op("pe", lambda te, c=c, ptv=ptv,
                                               dnp=dnp, nch=nch: te.matmul(
                                                   dnp, ones_bf[:], ptv,
                                                   start=(c == 0),
                                                   stop=(c == nch - 1)),
                                               waits=w_dn, tag="pe")
                                pt_war[c % 8] = ev_dnmm
                            if cc_i == 2 and pending_fin[0] is not None:
                                pending_fin[0]()
                                pending_fin[0] = None
                        k_readers.append(last_exp)
                        v_readers.append(ev_avmm)
                        q2_war[qpar] = [last_exp]

                        def fin(ev_avmm=ev_avmm, ev_dnmm=ev_dnmm, avp=avp,
                                dnp=dnp, avb=avb, h=h, J=J,
                                k_readers=k_readers, v_readers=v_readers):
                            nonlocal rden_war, rb3_war, rdenb_war
                            ed1 = p.op("dve", lambda v, dnp=dnp: v.reciprocal(
                                rden[:], dnp), waits=[ev_dnmm] + rden_war,
                                tag="dve")
                            dnp_war[avb] = [ed1]
                            ed2 = p.op("pe", lambda te: te.matmul(
                                ps_rb3[:], ones_f32r[:], rden[:],
                                start=True, stop=True),
                                waits=[ed1] + rb3_war, tag="pe")
                            rden_war = [ed2]
                            ed3 = p.op("dve", lambda v: v.tensor_copy(
                                rden_b[:], ps_rb3[:]),
                                waits=[ed2] + rdenb_war, tag="dve")
                            rb3_war = [ed3]
                            dst = arena[:, h * S + J * 512:
                                        h * S + (J + 1) * 512]
                            ee = p.op("dve", lambda v, avp=avp, dst=dst:
                                      v.tensor_tensor(out=dst, in0=avp,
                                                      in1=rden_b[:],
                                                      op=ALU.mult),
                                      waits=[ev_avmm, ed3], tag="dve")
                            av_war[avb] = [ee]
                            rdenb_war = [ee]
                            arena_evs.append(ee)
                        assert pending_fin[0] is None
                        pending_fin[0] = fin
                k_war[kpar] = k_readers
                v_war[kpar] = v_readers
            pending_fin[0]()
            pending_fin[0] = None

            lo = NH2 * 128 if NKV > 1 else 0
            ev_send = p.op("sp", lambda s, lo=lo: s.dma_start(
                out=send_s[lo:NH * 128, :].rearrange("(h p) s -> p h s", p=128),
                in_=arena[:, lo * S // 128:NH * S].rearrange(
                    "p (h s) -> p h s", s=S)),
                waits=list(arena_evs), tag=p.q("qsend"))
            cc_evs.append(p.op("pool", lambda g, lo=lo: g.collective_compute(
                "AllGather", ALU.bypass, replica_groups=groups,
                ins=[send_s[lo:NH * 128, :].opt()],
                outs=[recv_s[2 * lo:2 * NH * 128, :].opt()]),
                waits=[ev_send], tag=p.q1("qcc")))
            p.flush(block)
        p.set_barrier()

        if NKV > 1:
            perm = (list(range(0, NH2)) + list(range(NH, NH + NH2))
                    + list(range(NH2, NH)) + list(range(NH + NH2, 2 * NH)))
        else:
            perm = list(range(2 * NH))
        _emit_back(nc, p, block, cfg, consts, t, cc_evs, recv_t,
                   x_my, wo, rwT, rb, eb, y, x1_s, perm)

    return nc



def _emit_back(nc, p, block, cfg, consts, t, cc_evs, recv_t,
               x_my, wo, rwT, rb, eb, y, x1_s, perm):
    from contextlib import ExitStack
    S, D, ND, NH = (cfg[k] for k in ("S", "D", "ND", "NH"))
    SQ2, NT2, NDO = (cfg[k] for k in ("SQ2", "NT2", "NDO"))
    H = cfg["H"]
    E = E_EXPERTS
    invD = 1.0 / D
    if True:
        id_bf, id_f32 = t["id_bf"][:], t["id_f32"][:]
        ones_bf, ones_f32r = t["ones_bf"][:], t["ones_f32r"][:]
        ev_setup = t["ev_setup"]

        with ExitStack() as _st:
            arena = _st.enter_context(nc.sbuf_tensor("arena3", [128, H * SQ2], BF16))
            wo_sb = _st.enter_context(nc.sbuf_tensor("wo_sb", [128, 2 * ND * 512], BF16))
            xcol = _st.enter_context(nc.sbuf_tensor("xcol", [128, 3 * 512], F32))
            x1col = _st.enter_context(nc.sbuf_tensor("x1col", [128, 3 * 512], F32))
            ps_o = _st.enter_context(nc.psum_tensor("ps_o", [128, 2 * 512], F32))

            pid_sb = _st.enter_context(
                nc.sbuf_tensor("pid_sb", [1, 1], mybir.dt.uint32))
            rg = _st.enter_context(nc.gpsimd.register("rg_half"))
            ev_pid = p.op("pool", lambda g: g.dma_start(
                out=pid_sb[0:1, 0:1], in_=nc.partition_id_tensor.ap()),
                tag=p.q("qpid"))
            p.op("pool", lambda g: g.reg_load(rg, pid_sb[0:1, 0:1]),
                 waits=[ev_pid])
            p.op("pool", lambda g: g.reg_mod(rg, rg, 2))
            p.op("pool", lambda g: g.reg_mul(rg, rg, SQ2))
            ev_ar = p.op("pool", lambda g: g.dma_start(
                out=arena[:].rearrange("p (h s) -> p h s", s=SQ2),
                in_=bass.AP(recv_t, rg, [[S, 128], [128 * S, H], [1, SQ2]])),
                waits=list(cc_evs), tag=p.q("qar"))

            wo_v = wo.rearrange("(d p) c -> p d c", p=128)
            wo_war = {0: [], 1: []}
            xc_war = {0: [], 1: [], 2: []}
            x1_war = {0: [], 1: [], 2: []}
            o_war = {0: [], 1: []}
            for do in range(NDO):
                wpar = do % 2
                wsl = wo_sb[:, wpar * ND * 512:(wpar + 1) * ND * 512]
                ev_w = p.op("pool", lambda g, do=do, wsl=wsl: g.dma_start(
                    out=wsl.rearrange("p (d w) -> p d w", w=512),
                    in_=wo_v[:, :, do * 512:(do + 1) * 512]),
                    waits=wo_war[wpar] + ev_setup, tag=p.q(f"qwo{wpar}"))
                w_readers = []
                for tt in range(NT2):
                    par = (do * NT2 + tt) % 3
                    ob = (do * NT2 + tt) % 2
                    op_ = ps_o[:, ob * 512:ob * 512 + 512]
                    xcv = xcol[:, par * 512:(par + 1) * 512]
                    x1v = x1col[:, par * 512:(par + 1) * 512]
                    ev_xc = p.op("act", lambda s, tt=tt, do=do, xcv=xcv:
                                 s.dma_start(
                                     out=xcv,
                                     in_=x_my[tt * 128:(tt + 1) * 128,
                                              do * 512:(do + 1) * 512]),
                                 waits=xc_war[par], tag=p.q(f"qxc{par}"))
                    last = None
                    for hc in range(H):
                        w = [ev_ar, ev_w] + o_war[ob] if hc == 0 else []
                        last = p.op("pe", lambda te, hc=hc, tt=tt, op_=op_,
                                    wsl=wsl: te.matmul(
                                        op_,
                                        arena[:, hc * SQ2 + tt * 128:
                                              hc * SQ2 + tt * 128 + 128],
                                        wsl[:, perm[hc] * 512:
                                            perm[hc] * 512 + 512],
                                        start=(hc == 0), stop=(hc == H - 1)),
                                    waits=w, tag="pe" if hc == H - 1 else None)
                    ee = p.op("dve", lambda v, op_=op_, xcv=xcv, x1v=x1v:
                              v.tensor_tensor(out=x1v, in0=op_, in1=xcv,
                                              op=ALU.add),
                              waits=[last, ev_xc] + x1_war[par], tag="dve")
                    o_war[ob] = [ee]
                    es = p.op("sp", lambda s, tt=tt, do=do, x1v=x1v: s.dma_start(
                        out=x1_s[tt * 128:(tt + 1) * 128,
                                 do * 512:(do + 1) * 512],
                        in_=x1v), waits=[ee], tag=p.q(f"qx1{par}"))
                    x1_war[par] = [es]
                    xc_war[par] = [ee]
                    w_readers.append(last)
                wo_war[wpar] = w_readers
            p.flush(block)
        p.set_barrier()

        with ExitStack() as _st:
            x1t = _st.enter_context(nc.sbuf_tensor("x1t", [128, 2 * D], F32))
            x1bf = _st.enter_context(nc.sbuf_tensor("x1bf", [128, D], BF16))
            x1fm = _st.enter_context(nc.sbuf_tensor("x1fm", [128, ND * 128], BF16))
            rw_sb = _st.enter_context(nc.sbuf_tensor("rw_sb", [128, ND * 64], BF16))
            eb_sb = _st.enter_context(nc.sbuf_tensor("eb_sb", [64, D], BF16))
            rb_sb = _st.enter_context(nc.sbuf_tensor("rb_sb", [1, E], F32))
            rb_bc = _st.enter_context(nc.sbuf_tensor("rb_bc", [128, E], F32))
            s_rw = _st.enter_context(nc.sbuf_tensor("s_rw", [1, E], BF16))
            negmu2_bf = _st.enter_context(nc.sbuf_tensor("negmu2_bf", [1, 2 * 128], BF16))
            stat2 = _st.enter_context(nc.sbuf_tensor("stat2", [128, 16], F32))
            trash2 = _st.enter_context(nc.sbuf_tensor("trash2", [128, D], BF16))
            lgs = _st.enter_context(nc.sbuf_tensor("lgs", [128, E], F32))
            pex = _st.enter_context(nc.sbuf_tensor("pex", [128, E], F32))
            den = _st.enter_context(nc.sbuf_tensor("den", [128, 4], F32))
            t8 = _st.enter_context(nc.sbuf_tensor("t8", [128, 8], F32))
            repl = _st.enter_context(nc.sbuf_tensor("repl", [128, E], F32))
            mskb = _st.enter_context(nc.sbuf_tensor("mskb", [128, E], BF16))
            wdT = _st.enter_context(nc.sbuf_tensor("wdT", [64, 2 * 128], BF16))
            ln2 = _st.enter_context(nc.sbuf_tensor("ln2", [128, D], F32))
            t2 = _st.enter_context(nc.sbuf_tensor("t2", [128, D], F32))
            outt = _st.enter_context(nc.sbuf_tensor("outt", [128, 2 * D], F32))
            ps_tr2 = _st.enter_context(nc.psum_tensor("ps_tr2", [128, 1024], BF16))
            ps_r = _st.enter_context(nc.psum_tensor("ps_r", [128, 512], F32))
            ps_b = _st.enter_context(nc.psum_tensor("ps_b", [128, 2 * 512], F32))
            ps_s2 = _st.enter_context(nc.psum_tensor("ps_s2", [128, 512], F32))
            ps_wd = _st.enter_context(nc.psum_tensor("ps_wd", [64, 2 * 128], BF16))

            rw_v = rwT.rearrange("(d p) e -> p d e", p=128)
            ev_rw = p.op("pool", lambda g: g.dma_start(
                out=rw_sb[:].rearrange("p (d e) -> p d e", e=E), in_=rw_v),
                tag=p.q("qrw"))
            ev_eb = p.op("pool", lambda g: g.dma_start(
                out=eb_sb[:], in_=eb), tag=p.q("qeb"))
            ev_rb = p.op("sp", lambda s: s.dma_start(out=rb_sb[:], in_=rb),
                         tag=p.q("qrb"))
            lastc = None
            for d in range(ND):
                w = [ev_rw] + ev_setup if d == 0 else []
                lastc = p.op("pe", lambda te, d=d: te.matmul(
                    ps_s2[0:1, 0:E], ones_bf[:], rw_sb[:, d * E:(d + 1) * E],
                    start=(d == 0), stop=(d == ND - 1)),
                    waits=w, tag="pe" if d == ND - 1 else None)
            ev_srw = p.op("dve", lambda v: v.tensor_copy(
                s_rw[:], ps_s2[0:1, 0:E]), waits=[lastc], tag="dve")
            eb1 = p.op("pe", lambda te: te.matmul(
                ps_r[:, 0:E], ones_f32r[:], rb_sb[:], start=True, stop=True),
                waits=[ev_rb] + ev_setup, tag="pe")
            ev_rbc = p.op("dve", lambda v: v.tensor_copy(
                rb_bc[:], ps_r[:, 0:E]), waits=[eb1], tag="dve")

            x1_war = {0: [], 1: []}
            tr2_war = [None]
            st_war = [ev_srw]
            r_war = [ev_rbc]
            lgs_war = []
            pex_war = []
            repl_war = []
            mskb_war = []
            t8_war = []
            den_war = {0: [], 1: []}
            rd_war = {0: [], 1: []}
            wd_war = {0: [], 1: []}
            pwd_war = {0: [], 1: []}
            b_war = {0: [], 1: []}
            out_war = {0: [], 1: []}
            ln2_war = []
            t2_war = []
            x1bf_war = []
            x1fm_war = []
            trash2_war = []
            mx_war = {0: [], 1: []}

            for tt in range(NT2):
                par = tt % 2
                xv = x1t[:, par * D:(par + 1) * D]
                ev_l = p.op("sp", lambda s, tt=tt, xv=xv: s.dma_start(
                    out=xv, in_=x1_s[tt * 128:(tt + 1) * 128, :]),
                    waits=x1_war[par], tag=p.q(f"qxt{par}"))
                c_sum = stat2[:, par * 8 + 0: par * 8 + 1]
                c_sq = stat2[:, par * 8 + 1: par * 8 + 2]
                c_negmu = stat2[:, par * 8 + 2: par * 8 + 3]
                c_mu2 = stat2[:, par * 8 + 3: par * 8 + 4]
                c_var = stat2[:, par * 8 + 4: par * 8 + 5]
                c_sd = stat2[:, par * 8 + 5: par * 8 + 6]
                c_r = stat2[:, par * 8 + 6: par * 8 + 7]
                c_mx = stat2[:, par * 8 + 7: par * 8 + 8]
                e1 = p.op("dve", lambda v, a=c_sum, b=xv: v.reduce_sum(
                    a, b, axis=AXX), waits=[ev_l], tag="dve")
                e2 = p.op("act", lambda s, a=c_sq, b=xv: s.activation(
                    out=trash2[:], in_=b, func=AF.Square, accum_out=a),
                    waits=[ev_l] + trash2_war, tag="act")
                trash2_war = [e2]
                e3 = p.op("dve", lambda v, a=c_negmu, b=c_sum:
                          v.tensor_scalar_mul(a, b, -invD), waits=[e1],
                          tag="dve")
                e4 = p.op("dve", lambda v, a=c_mu2, b=c_negmu: v.tensor_mul(
                    a, b, b), waits=[e3], tag="dve")
                e5 = p.op("dve", lambda v, a=c_var, b=c_sq, c=c_mu2:
                          v.scalar_tensor_tensor(out=a, in0=b, scalar=invD,
                                                 in1=c, op0=ALU.mult,
                                                 op1=ALU.subtract),
                          waits=[e2, e4], tag="dve")
                e6 = p.op("act", lambda s, a=c_sd, b=c_var: s.activation(
                    out=a, in_=b, func=AF.Sqrt, bias=consts[3]),
                    waits=[e5], tag="act")
                e7 = p.op("dve", lambda v, a=c_r, b=c_sd: v.reciprocal(a, b),
                          waits=[e6], tag="dve")
                e8 = p.op("pe", lambda te, b=c_negmu: te.transpose(
                    ps_s2[0:1, 0:128], b, id_f32), waits=[e3] + st_war,
                    tag="pe")
                e9 = p.op("dve", lambda v, par=par: v.tensor_copy(
                    negmu2_bf[0:1, par * 128:(par + 1) * 128],
                    ps_s2[0:1, 0:128]), waits=[e8], tag="dve")
                st_war = [e9]
                e10 = p.op("dve", lambda v, xv=xv: v.tensor_copy(x1bf[:], xv),
                           waits=[ev_l] + x1bf_war, tag="dve")
                tr_evs = []
                x1bf_war = []
                NB8 = min(8, ND)
                for bk in range(ND // NB8):
                    bank = ps_tr2[:, 0:NB8 * 128]
                    lt = None
                    for dd in range(NB8):
                        d = bk * NB8 + dd
                        w = [e10]
                        if dd == 0 and tr2_war[0] is not None:
                            w.append(tr2_war[0])
                        lt = p.op("pe", lambda te, d=d, dd=dd, bank=bank:
                                  te.transpose(
                                      bank[:, dd * 128:(dd + 1) * 128],
                                      x1bf[:, d * 128:(d + 1) * 128], id_bf),
                                  waits=w, tag="pe" if dd == NB8 - 1 else None)
                    dst = x1fm[:].rearrange("p (d w) -> p d w", w=128)[
                        :, bk * NB8:(bk + 1) * NB8, :]
                    eev = p.op("act", lambda s, bank=bank, dst=dst: s.copy(
                        out=dst, in_=bank.rearrange("p (d w) -> p d w", w=128)),
                        waits=[lt] + x1fm_war, tag="act")
                    tr2_war[0] = eev
                    tr_evs.append(eev)
                    x1bf_war.append(lt)
                lastr = None
                for d in range(ND):
                    w = tr_evs + [ev_rw] + r_war if d == 0 else []
                    lastr = p.op("pe", lambda te, d=d: te.matmul(
                        ps_r[:, 0:E], x1fm[:, d * 128:(d + 1) * 128],
                        rw_sb[:, d * E:(d + 1) * E],
                        start=(d == 0), stop=False), waits=w)
                ef = p.op("pe", lambda te, par=par: te.matmul(
                    ps_r[:, 0:E], negmu2_bf[0:1, par * 128:(par + 1) * 128],
                    s_rw[0:1, :], start=False, stop=True),
                    waits=[ev_srw, e9], tag="pe")
                x1fm_war = [ef]
                eg = p.op("dve", lambda v, c=c_r: v.scalar_tensor_tensor(
                    out=lgs[:], in0=ps_r[:, 0:E], scalar=c, in1=rb_bc[:],
                    op0=ALU.mult, op1=ALU.add),
                    waits=[ef, e7, ev_rbc] + lgs_war, tag="dve")
                r_war = [eg]
                eh1 = p.op("dve", lambda v, a=c_mx: v.reduce_max(
                    a, lgs[:], axis=AXX, negate=True),
                    waits=[eg] + mx_war[par], tag="dve")
                eh2 = p.op("act", lambda s, a=c_mx, dpar=par: s.activation(
                    out=pex[:], in_=lgs[:], func=AF.Exp, bias=a,
                    accum_out=den[:, dpar:dpar + 1]),
                    waits=[eh1] + pex_war + den_war[par], tag="act")
                mx_war[par] = [eh2]
                eh3 = p.op("dve", lambda v: v.max(out=t8[:], in_=pex[:]),
                           waits=[eh2] + t8_war, tag="dve")
                eh4 = p.op("dve", lambda v: v.match_replace(
                    out=repl[:], in_to_replace=t8[:], in_values=pex[:],
                    imm_value=0.0), waits=[eh3] + repl_war, tag="dve")
                c_w8 = stat2[:, par * 8 + 3: par * 8 + 4]
                eh5 = p.op("dve", lambda v, a=c_w8: v.reduce_sum(
                    a, t8[:], axis=AXX), waits=[eh4, e5], tag="dve")
                eh6 = p.op("dve", lambda v: v.tensor_sub(
                    mskb[:], pex[:], repl[:]), waits=[eh4] + mskb_war,
                    tag="dve")
                lgs_war = [eh1, eh2]
                pex_war = [eh3, eh4, eh6]
                repl_war = [eh6]
                t8_war = [eh4, eh5]
                wdv = wdT[:, par * 128:(par + 1) * 128]
                pwd = ps_wd[:, par * 128:(par + 1) * 128]
                eh7 = p.op("pe", lambda te, pwd=pwd: te.transpose(
                    pwd, mskb[:], id_bf),
                    waits=[eh6] + wd_war[par] + pwd_war[par], tag="pe")
                mskb_war = [eh7]
                eh8 = p.op("dve", lambda v, pwd=pwd, wdv=wdv: v.tensor_copy(
                    wdv, pwd), waits=[eh7], tag="dve")
                pwd_war[par] = [eh8]
                c_rd = stat2[:, par * 8 + 4: par * 8 + 5]
                ei1 = p.op("dve", lambda v, a=c_rd, dpar=par: v.reciprocal(
                    a, den[:, dpar:dpar + 1]),
                    waits=[eh2, e6] + rd_war[par], tag="dve")
                c_wn = stat2[:, par * 8 + 0: par * 8 + 1]
                ei2 = p.op("dve", lambda v, a=c_wn, b=c_w8, c=c_rd:
                           v.tensor_mul(a, b, c),
                           waits=[eh5, ei1, e3], tag="dve")
                ej1 = p.op("dve", lambda v, a=c_negmu, b=c_r, xv=xv:
                           v.tensor_scalar(out=ln2[:], in0=xv, scalar1=a,
                                           scalar2=b, op0=ALU.add,
                                           op1=ALU.mult),
                           waits=[e7, eg] + ln2_war + out_war[par], tag="dve")
                ej2 = p.op("dve", lambda v, a=c_wn, xv=xv:
                           v.scalar_tensor_tensor(out=t2[:], in0=ln2[:],
                                                  scalar=a, in1=xv,
                                                  op0=ALU.mult, op1=ALU.add),
                           waits=[ej1, ei2] + t2_war, tag="dve")
                ln2_war = [ej2]
                ov = outt[:, par * D:(par + 1) * D]
                out_evs = []
                for do in range(NDO):
                    bb = (tt * NDO + do) % 2
                    pb = ps_b[:, bb * 512:bb * 512 + 512]
                    w = [eh8, ev_eb] + b_war[bb]
                    em = p.op("pe", lambda te, do=do, pb=pb, wdv=wdv: te.matmul(
                        pb, wdv, eb_sb[:, do * 512:(do + 1) * 512],
                        start=True, stop=True), waits=w, tag="pe")
                    eo = p.op("dve", lambda v, do=do, pb=pb, ov=ov, c=c_rd:
                              v.scalar_tensor_tensor(
                                  out=ov[:, do * 512:(do + 1) * 512], in0=pb,
                                  scalar=c,
                                  in1=t2[:, do * 512:(do + 1) * 512],
                                  op0=ALU.mult, op1=ALU.add),
                              waits=[em, ej2, ei1], tag="dve")
                    b_war[bb] = [eo]
                    out_evs.append(eo)
                t2_war = out_evs
                wd_war[par] = [em]
                den_war[par] = [ei1, ei2]
                es = p.op("sp", lambda s, tt=tt, ov=ov: s.dma_start(
                    out=y[tt * 128:(tt + 1) * 128, :], in_=ov),
                    waits=out_evs, tag=p.q(f"qy{par}"))
                out_war[par] = [es]
                rd_war[par] = out_evs
                x1_war[par] = [e1, e2, e5, e10, ej1, ej2]

            p.op("pool", lambda g: g.engine_nop(),
                 waits=[("qy0", p.count["qy0"]),
                        ("qy1", p.count.get("qy1", 0))]
                 if p.count.get("qy1", 0) else [("qy0", p.count["qy0"])],
                 tag="pool")
            p.flush(block)



def _run(cfg, inputs, n_cores=8, runner=None):
    cfg_d = _derive(dict(cfg, B=n_cores // 2))
    S, D = cfg["S"], cfg["D"]
    NH, NKV, SQ2 = cfg_d["NH"], cfg_d["NKV"], cfg_d["SQ2"]
    X = np.ascontiguousarray(inputs["X"], dtype=np.float32)
    Wq = np.ascontiguousarray(inputs["Wq"], dtype=np.float32)
    Wk = np.ascontiguousarray(inputs["Wk"], dtype=np.float32)
    Wv = np.ascontiguousarray(inputs["Wv"], dtype=np.float32)
    Wo = np.ascontiguousarray(inputs["Wo"], dtype=np.float32)
    RW = np.ascontiguousarray(inputs["RW"], dtype=np.float32)
    RB = np.ascontiguousarray(inputs["RB"], dtype=np.float32).reshape(1, -1)
    EB = np.ascontiguousarray(inputs["EB"], dtype=np.float32)
    B = X.shape[0]
    assert n_cores == 2 * B
    if runner is None:
        runner = lambda nc_, maps: run_bass_kernel_spmd(
            nc_, maps, core_ids=list(range(n_cores))).results

    nc = build_fused(dict(cfg, B=B))
    rwT = np.ascontiguousarray(RW.T)
    in_maps = []
    for c in range(n_cores):
        b, half = c // 2, c % 2
        in_maps.append({
            "x_in": X[b],
            "wq": np.ascontiguousarray(
                Wq[:, half * NH * 128:(half + 1) * NH * 128]),
            "wk": np.ascontiguousarray(
                Wk[:, half * NKV * 128:(half + 1) * NKV * 128]),
            "wv": np.ascontiguousarray(
                Wv[:, half * NKV * 128:(half + 1) * NKV * 128]),
            "x_my": np.ascontiguousarray(X[b, half * SQ2:(half + 1) * SQ2, :]),
            "wo": Wo,
            "rwT": rwT,
            "rb": RB,
            "eb": EB,
        })
    res = runner(nc, in_maps)
    ys = [r["y"] for r in res]
    out = np.empty((B, S, D), dtype=np.float32)
    for c in range(n_cores):
        b, half = c // 2, c % 2
        out[b, half * SQ2:(half + 1) * SQ2, :] = ys[c]
    return out


def kernel(L0_attn_X, L0_attn_Wq, L0_attn_Wk, L0_attn_Wv, L0_attn_Wo,
           L0_ffn_router_weight, L0_ffn_router_bias, L0_ffn_expert_bias):
    inputs = dict(X=L0_attn_X, Wq=L0_attn_Wq, Wk=L0_attn_Wk, Wv=L0_attn_Wv,
                  Wo=L0_attn_Wo, RW=L0_ffn_router_weight,
                  RB=L0_ffn_router_bias, EB=L0_ffn_expert_bias)
    return _run(REAL, inputs, n_cores=8)
